# version 09 (final)
# baseline (speedup 1.0000x reference)
import numpy as np
import contextlib
import concourse.bass as bass
import concourse.mybir as mybir
from concourse.bass_utils import run_bass_kernel_spmd

F32 = mybir.dt.float32
BF16 = mybir.dt.bfloat16
AF = mybir.ActivationFunctionType
ALU = mybir.AluOpType
AX = mybir.AxisListType

NCORES = 8
B_CORE = 4
L = 2048
H = 1024
NH = 8
HID2 = 2048
NE = 8
EHID = 2048
G = 512
NGRP = B_CORE * L // G
GPS = L // G
CPG = G // 128
TOK = B_CORE * L
LN_EPS = 1e-5
RMS_EPS = 1e-5
FFN_EPS = 1e-6
L2_EPS = 1e-6
NBLK = 12 + NE * 12


class Prog:
    def __init__(self):
        self.ops = []
        self.names = []
        self.counts = []

    def sem(self, name):
        i = len(self.names)
        self.names.append(name)
        self.counts.append(0)
        return i

    def now(self, s):
        return (s, self.counts[s])

    def op(self, eng, fn, waits=(), inc=None, dma=False):
        incs = []
        if inc is not None:
            for s in ([inc] if isinstance(inc, int) else inc):
                d = 16 if dma else 1
                self.counts[s] += d
                incs.append((s, d))
        self.ops.append((eng, fn, [w for w in waits if w is not None], incs))
        if incs:
            return (incs[0][0], self.counts[incs[0][0]])
        return None

    def replay(self, eng_name, eng, sems):
        hw = {}
        for (e, fn, waits, incs) in self.ops:
            if e != eng_name:
                continue
            best = {}
            for (s, v) in waits:
                if v > 0 and v > hw.get(s, 0) and v > best.get(s, 0):
                    best[s] = v
            for s, v in best.items():
                eng.wait_ge(sems[s], v)
                hw[s] = v
            inst = fn(eng)
            for (s, d) in incs:
                inst.then_inc(sems[s], d)


def ts(i, n=128):
    return slice(i * n, (i + 1) * n)


def build(pipelined=True):
    nc = bass.Bass("TRN2", target_bir_lowering=False)
    p = Prog()

    dx = nc.dram_tensor("x", [TOK, H], F32, kind="ExternalInput").ap()
    dout = nc.dram_tensor("out", [TOK, H], F32, kind="ExternalOutput").ap()
    dW = {}
    for nm, shp in [("Wq", [H, H]), ("Wk", [H, H]), ("Wv", [H, H]),
                    ("Wo", [H, H]), ("fus_W1", [H, HID2]),
                    ("fus_W2", [HID2, 32]), ("Wb", [H, NH]),
                    ("floor_W", [H, NH]), ("w_routerT", [H, NE]),
                    ("w_upT", [NE, H, 2 * EHID]), ("w_downT", [NE, EHID, H]),
                    ("conv_q_pt", [128, 32]), ("conv_k_pt", [128, 32]),
                    ("conv_v_pt", [128, 32]), ("fir_s_pt", [128, 24]),
                    ("fir_l_pt", [128, 248]), ("fus_b1_pt", [128, 16]),
                    ("fus_b2_c", [32, 1]), ("floor_b_c", [NH, 1]),
                    ("norm_pf_r", [1, 128]), ("norm_o_r", [1, 128]),
                    ("gamma_r", [1, H])]:
        dW[nm] = nc.dram_tensor(nm, shp, F32, kind="ExternalInput").ap()
    scr = nc.dram_tensor("scr", [NBLK, 128, 8, G], BF16, kind="Internal").ap()

    sCT = p.sem("const_dma")
    sXp = [p.sem("x_dma0"), p.sem("x_dma1")]
    sLA = [p.sem("eload0"), p.sem("eload1")]
    sLM = [p.sem("mload0"), p.sem("mload1")]
    sST = [p.sem("a1st0"), p.sem("a1st1")]
    sSE = [p.sem("epist0"), p.sem("epist1")]
    sSM = [p.sem("moest0"), p.sem("moest1")]
    sARs = [p.sem("aring0"), p.sem("aring1")]
    sMRs = [p.sem("mring0"), p.sem("mring1"), p.sem("mring2")]
    sCS = [p.sem(f"convst{i}") for i in range(5)]
    sWD = p.sem("wd_dma")
    sPE = p.sem("pe")
    sDV = p.sem("dve")
    sAC = p.sem("act")
    sPO = p.sem("pool")

    stack = contextlib.ExitStack()
    sb = lambda name, shape, dt: stack.enter_context(nc.sbuf_tensor(name, shape, dt))
    psa = lambda name, shape, dt: stack.enter_context(nc.psum_tensor(name, shape, dt))

    with stack:
        c_cw = {"Wq": sb("cwq", [128, 32], F32), "Wk": sb("cwk", [128, 32], F32),
                "Wv": sb("cwv", [128, 32], F32)}
        c_fs = sb("firs", [128, 24], F32)
        c_fl = sb("firl", [128, 248], F32)
        c_b1 = sb("fusb1", [128, 16], F32)
        c_b2 = sb("fusb2", [32, 1], F32)
        c_fb = sb("floorb", [NH, 1], F32)
        c_ones = sb("onesrow", [1, 128], F32)
        c_id16 = sb("id16", [128, 128], BF16)
        c_idf = sb("idf", [128, 128], F32)
        c_mslu = sb("mslu", [128, 2, 128], F32)
        c_miu = sb("incl_up", [128, 128], F32)
        c_pfb = sb("pf_b", [128, 128], F32)
        c_nob = sb("no_b", [128, 128], F32)
        c_gmb = sb("gm_b", [128, H], F32)
        r_wb = sb("r_wb", [128, 8, NH], BF16)
        r_wf = sb("r_wf", [128, 8, NH], BF16)
        r_wr = sb("r_wr", [128, 8, NE], F32)
        r_w2 = sb("r_w2", [128, 16, 32], BF16)

        x_t = sb("x_t", [128, 1, H], F32)
        xn_tok = sb("xn_tok", [128, 1, H], F32)
        xr_tok = sb("xr_tok", [128, 1, H], F32)
        dump = sb("dump", [128, 512], F32)
        stat = sb("stat", [128, 2, 16], F32)
        ext = sb("ext", [128, 64], F32)
        outp = sb("outp", [128, 4, H], F32)
        xn_fm = sb("xn_fm", [128, 8, G], BF16)
        xr_fm = sb("xr_fm", [128, 2, 8, G], BF16)
        w8 = sb("w8", [128, 2, CPG, NE], F32)
        beta_b = sb("beta_b", [128, CPG, NH], F32)
        coef = sb("coef", [128, CPG, 4, NH], F32)
        sc_fm = sb("sc_fm", [NE, 128], F32)
        bf_fm = sb("bf_fm", [NH, 2, G], BF16)
        fus_tok = sb("fus_tok", [128, 32], F32)
        arena2 = sb("arena2", [128, 4096], mybir.dt.uint8)
        xr_fm32 = arena2.ap()[:, :].bitcast(F32).rearrange(
            "p (a b) -> p a b", a=8)
        fus_fm = arena2.ap()[0:32, 0:2048].bitcast(F32)

        raw = sb("raw", [128, 8, 4 + G], BF16)
        tails = sb("tails", [128, 3, 8, 4], BF16)
        cacc = sb("cacc", [128, 2, G], F32)
        q_fm = sb("q_fm", [128, 8, G], BF16)
        k_fm = sb("k_fm", [128, 8, G], BF16)
        v_fir = sb("v_fir", [128, 8, 32 + G], BF16)
        fir_s = sb("fir_s", [128, 8, G], BF16)

        aring = [sb(f"aring{i}", [128, 8, G], BF16) for i in range(2)]
        mring = [sb(f"mring{i}", [128, 8, G], BF16) for i in range(3)]

        S32 = sb("S32", [128, NH, 128], F32)
        s16t = sb("s16t", [128, 128], BF16)
        dstat = sb("dstat", [128, 16], F32)
        d_tok = sb("d_tok", [128, 3, 128], BF16)
        d_fm3 = sb("d_fm3", [128, 3, 128], BF16)
        d_knt = sb("d_knt", [128, 128], BF16)
        d_kbt = sb("d_kbt", [128, 128], BF16)
        d_vbt = sb("d_vbt", [128, 128], BF16)
        d_AAT = sb("d_AAT", [128, 2, 128], BF16)
        d_PP = sb("d_PP", [128, 5, 2, 128], BF16)
        d_T = sb("d_T", [128, 2, 128], BF16)
        d_uw = sb("d_uw", [128, 2, 128], BF16)
        d_alT = sb("d_alT", [128, 128], BF16)
        d_unew = sb("d_unew", [128, 128], BF16)
        d_o2 = sb("d_o2", [128, 128], BF16)
        d_ft = sb("d_ft", [128, 2, 128], BF16)
        o_tmp = sb("o_tmp", [128, 128], F32)
        o2_fm = sb("o2_fm", [128, 8, 128], BF16)

        hid = sb("hid", [128, 16, G], BF16)
        acc = sb("acc", [128, CPG, H], F32)
        sg_ = sb("sg", [128, 2, G], BF16)

        pa = [psa(f"pa{i}", [128, 512], F32) for i in range(3)]
        pt_a = psa("pta", [128, 1024], BF16)
        pm = [psa(f"pm{i}", [128, 512], F32) for i in range(4)]

        ctx = {}

        def bankw(key):
            return ctx.get(("bk", key))

        def bankr(key, ev):
            ctx[("bk", key)] = ev

        def ld(dst, src, waits=(), sem=None):
            return p.op("sync", lambda e, d=dst, s=src: e.dma_start(out=d, in_=s),
                        waits=waits, inc=(sCT if sem is None else sem), dma=True)

        def ldw(dst, src, waits=(), sem=None):
            return p.op("pool", lambda e, d=dst, s=src: e.dma_start(out=d, in_=s),
                        waits=waits, inc=(sWD if sem is None else sem), dma=True)

        def st_dma(dst, src, waits=(), sem=None):
            return p.op("sync", lambda e, d=dst, s=src: e.dma_start(out=d, in_=s),
                        waits=waits, inc=(sCT if sem is None else sem), dma=True)

        ld(c_cw["Wq"][:, :], dW["conv_q_pt"])
        ld(c_cw["Wk"][:, :], dW["conv_k_pt"])
        ld(c_cw["Wv"][:, :], dW["conv_v_pt"])
        ld(c_fs[:, :], dW["fir_s_pt"])
        ld(c_fl[:, :], dW["fir_l_pt"])
        ld(c_b1[:, :], dW["fus_b1_pt"])
        ld(c_b2[:, :], dW["fus_b2_c"])
        ld(c_fb[:, :], dW["floor_b_c"])
        ld(dump[0:1, 0:128], dW["norm_pf_r"])
        ld(dump[0:1, 128:256], dW["norm_o_r"])
        ld(cacc[0:1, :, :].rearrange("p a b -> p (a b)"), dW["gamma_r"])
        ld(r_wr[:, :, :], dW["w_routerT"].rearrange("(f p) e -> p f e", p=128))
        ldw(r_wb[:, :, :], dW["Wb"].rearrange("(f p) e -> p f e", p=128))
        ldw(r_wf[:, :, :], dW["floor_W"].rearrange("(f p) e -> p f e", p=128))
        ldw(r_w2[:, :, :], dW["fus_W2"].rearrange("(f p) e -> p f e", p=128))
        w_pre = [p.now(sCT), p.now(sWD)]

        def POOL(fn, waits=()):
            return p.op("pool", fn, waits=list(waits) + [p.now(sPO)], inc=sPO)

        POOL(lambda e: e.memset(c_id16[:, :], 0.0))
        POOL(lambda e: e.affine_select(
            out=c_id16[:, :], in_=c_id16[:, :], compare_op=ALU.not_equal,
            fill=1.0, base=0, pattern=[[-1, 128]], channel_multiplier=1))
        POOL(lambda e: e.memset(c_idf[:, :], 0.0))
        POOL(lambda e: e.affine_select(
            out=c_idf[:, :], in_=c_idf[:, :], compare_op=ALU.not_equal,
            fill=1.0, base=0, pattern=[[-1, 128]], channel_multiplier=1))
        POOL(lambda e: e.memset(c_mslu[:, :, :], 0.0))
        POOL(lambda e: e.affine_select(
            out=c_mslu[:, 0, :], in_=c_mslu[:, 0, :], compare_op=ALU.is_ge,
            fill=-1.0, base=0, pattern=[[1, 128]], channel_multiplier=-1))
        POOL(lambda e: e.affine_select(
            out=c_mslu[:, 1, :], in_=c_mslu[:, 1, :], compare_op=ALU.is_ge,
            fill=-1.0, base=0, pattern=[[-1, 128]], channel_multiplier=1))
        POOL(lambda e: e.memset(c_miu[:, :], 0.0))
        POOL(lambda e: e.affine_select(
            out=c_miu[:, :], in_=c_miu[:, :], compare_op=ALU.is_gt,
            fill=1.0, base=0, pattern=[[-1, 128]], channel_multiplier=1))
        POOL(lambda e: e.memset(c_ones[:, :], 1.0))
        POOL(lambda e: e.memset(v_fir[:, :, :], 0.0))
        POOL(lambda e: e.memset(raw[:, :, :], 0.0))
        POOL(lambda e: e.memset(tails[:, :, :, :], 0.0))
        w_msk = POOL(lambda e: e.memset(S32[:, :, :], 0.0))
        for h in range(NH):
            ctx[("S32", h)] = w_msk

        def MM(out_ap, lhsT, rhs, waits=(), start=True, stop=True):
            return p.op("pe", lambda e: e.matmul(out_ap, lhsT, rhs,
                                                 start=start, stop=stop,
                                                 skip_group_check=True),
                        waits=waits, inc=sPE if stop else None)

        def PET(out_ap, in_ap, waits=(), f32=False):
            kp = in_ap.shape[0]
            ident = (c_idf if f32 else c_id16)[0:kp, 0:kp]
            return p.op("pe", lambda e: e.transpose(out_ap, in_ap, ident),
                        waits=waits, inc=sPE)

        def ACT(out_ap, in_ap, func=AF.Copy, waits=(), bias=0.0, scale=1.0,
                accum=None):
            if accum is not None:
                return p.op("act", lambda e: e.activation(
                    out_ap, in_ap, func, bias=bias, scale=scale,
                    accum_out=accum), waits=waits, inc=sAC)
            return p.op("act", lambda e: e.activation(
                out_ap, in_ap, func, bias=bias, scale=scale),
                waits=waits, inc=sAC)

        def DVE(fn, waits=()):
            return p.op("dve", fn, waits=waits, inc=sDV)

        def bcast(dst, src_row, n):
            for j in range(0, n, 512):
                w = min(512, n - j)
                wm = MM(pm[0][:, 0:w], c_ones[:, :], src_row[:, j:j + w],
                        waits=w_pre + [w_msk, bankw("m0")])
                ev = DVE(lambda e, j=j, w=w, d=dst: e.tensor_copy(
                    d[:, j:j + w], pm[0][:, 0:w]), waits=[wm])
                bankr("m0", ev)
        bcast(c_pfb, dump[0:1, 0:128], 128)
        bcast(c_nob, dump[0:1, 128:256], 128)
        bcast(c_gmb, cacc[0:1, :, :].rearrange("p a b -> p (a b)"), H)
        w_const = [p.now(sDV), w_msk, p.now(sCT), p.now(sWD)]

        ABLK = {"Wq": 0, "Wk": 2, "Wv": 4, "W1": 6, "Wo": 10}

        def conv_src(blk):
            if blk < 12:
                if blk < 6:
                    nm = ["Wq", "Wq", "Wk", "Wk", "Wv", "Wv"][blk]
                    half = blk % 2
                    return dW[nm].rearrange("(f p) n -> p f n", p=128)[
                        :, :, half * 512:(half + 1) * 512]
                if blk < 10:
                    half = blk - 6
                    return dW["fus_W1"].rearrange("(f p) n -> p f n", p=128)[
                        :, :, half * 512:(half + 1) * 512]
                half = blk - 10
                return dW["Wo"].rearrange("(f p) n -> p f n", p=128)[
                    :, :, half * 512:(half + 1) * 512]
            e_, j = divmod(blk - 12, 12)
            if j < 8:
                sp, ug = divmod(j, 2)
                src = dW["w_upT"][e_].rearrange("(f p) n -> p f n", p=128)
                off = ug * 2048 + sp * 512
                return src[:, :, off:off + 512]
            dh, kfh = divmod(j - 8, 2)
            src = dW["w_downT"][e_].rearrange("(f p) n -> p f n", p=128)
            return src[:, kfh * 8:(kfh + 1) * 8, dh * 512:(dh + 1) * 512]

        scr_ready = {}

        def conv_one(blk):
            if blk < 12:
                si = blk % 2
                buf, sem, key = aring[si], sARs[si], ("aring", si)
                ssem = sCS[si]
            else:
                si = (blk - 12) % 3
                buf, sem, key = mring[si], sMRs[si], ("mring", si)
                ssem = sCS[2 + si]
            lw = ldw(buf[:, :, :], conv_src(blk),
                     waits=[ctx.get(key)], sem=sem)
            stv = st_dma(scr[blk], buf[:, :, :], waits=[lw], sem=ssem)
            ctx[key] = stv
            scr_ready[blk] = stv

        for blk in range(12):
            conv_one(blk)

        def conv_gen():
            for blk in range(12, NBLK):
                conv_one(blk)
                yield

        def aload(blk):
            si = ctx.setdefault("a_next", 0)
            ctx["a_next"] = (si + 1) % 2
            ev = ldw(aring[si][:, :, :], scr[blk],
                     waits=[ctx.get(("aring", si)), scr_ready.get(blk)],
                     sem=sARs[si])
            return si, ev

        def moe_gen(g, attn_store):
            par = g % 2
            w_xr_all = [ctx[("xr16", par, cc)] for cc in range(CPG)]

            def mload(i, blk):
                si = i % 3
                ev = ldw(mring[si][:, :, :], scr[blk],
                         waits=[ctx.get(("mring", si)), scr_ready.get(blk)],
                         sem=sMRs[si])
                ctx[("mring", si)] = None
                return si, ev

            blks = []
            for e_ in range(NE):
                base = 12 + e_ * 12
                blks += [base + j for j in range(12)]
            loaded = {}
            nload = 0

            def ensure(i):
                nonlocal nload
                while nload <= i:
                    loaded[nload] = mload(nload, blks[nload])
                    nload += 1

            def free_slot(i, ev):
                ctx[("mring", i % 3)] = ev

            gi = 0
            for e_ in range(NE):
                last_eh = None
                for sp in range(4):
                    iu, ig = gi, gi + 1
                    ensure(ig + 1 if gi + 2 < len(blks) else ig)
                    su, evu = loaded[iu]
                    sg2, evg = loaded[ig]
                    for d128 in range(4):
                        fbh = sp * 4 + d128
                        bu, bg = fbh % 2, 2 + fbh % 2
                        wmU = None
                        for kf in range(8):
                            wmU = MM(pm[bu][:, :],
                                     mring[su][:, kf, ts(d128)],
                                     xr_fm[:, par, kf, :],
                                     waits=w_xr_all + [evu, bankw(f"m{bu}")],
                                     start=(kf == 0), stop=(kf == 7))
                        wmG = None
                        for kf in range(8):
                            wmG = MM(pm[bg][:, :],
                                     mring[sg2][:, kf, ts(d128)],
                                     xr_fm[:, par, kf, :],
                                     waits=w_xr_all + [evg, bankw(f"m{bg}")],
                                     start=(kf == 0), stop=(kf == 7))
                        esg = ACT(sg_[:, fbh % 2, :], pm[bg][:, :], AF.Silu,
                                  waits=[wmG, ctx.get(("sg_free", fbh % 2))])
                        bankr(f"m{bg}", esg)
                        eh = DVE(lambda e, fbh=fbh, bu=bu: e.tensor_tensor(
                            hid[:, fbh, :], pm[bu][:, :], sg_[:, fbh % 2, :],
                            ALU.mult),
                            waits=[wmU, esg, ctx.get(("hid_free", fbh))])
                        bankr(f"m{bu}", eh)
                        ctx[("sg_free", fbh % 2)] = eh
                        last_eh = eh
                        yield
                    pe_done = p.now(sPE)
                    free_slot(iu, pe_done)
                    free_slot(ig, pe_done)
                    gi += 2
                    ensure(min(gi + 2, len(blks) - 1))
                w_h_all = [last_eh]
                for dh in range(2):
                    iA, iB = gi, gi + 1
                    ensure(min(iB + 1, len(blks) - 1))
                    sA, evA = loaded[iA]
                    sB, evB = loaded[iB]
                    for c in range(CPG):
                        bk = c % 2
                        wm = None
                        for kf in range(8):
                            wm = MM(pm[bk][:, :], hid[:, kf, ts(c)],
                                    mring[sA][:, kf, :],
                                    waits=w_h_all + [evA, bankw(f"m{bk}")],
                                    start=(kf == 0), stop=False)
                        for kf in range(8):
                            wm = MM(pm[bk][:, :], hid[:, 8 + kf, ts(c)],
                                    mring[sB][:, kf, :],
                                    waits=[evB],
                                    start=False, stop=(kf == 7))
                        wcol = w8[:, par, c, e_:e_ + 1]
                        if e_ == 0:
                            ev = DVE(lambda e, c=c, dh=dh, bk=bk, wcol=wcol:
                                     e.tensor_scalar(
                                         acc[:, c, dh * 512:(dh + 1) * 512],
                                         pm[bk][:, :], wcol, None, ALU.mult),
                                     waits=[wm, ctx[("w8", par, c)],
                                            ctx.get(("acc_free", c))])
                        else:
                            ev = DVE(lambda e, c=c, dh=dh, bk=bk, wcol=wcol:
                                     e.scalar_tensor_tensor(
                                         acc[:, c, dh * 512:(dh + 1) * 512],
                                         pm[bk][:, :], wcol,
                                         acc[:, c, dh * 512:(dh + 1) * 512],
                                         ALU.mult, ALU.add),
                                     waits=[wm, ctx[("w8", par, c)],
                                            ctx.get(("accw", c, dh))])
                        ctx[("accw", c, dh)] = ev
                        bankr(f"m{bk}", ev)
                        yield
                    pe_done = p.now(sPE)
                    free_slot(iA, pe_done)
                    free_slot(iB, pe_done)
                    gi += 2
                for fbh in range(16):
                    ctx[("hid_free", fbh)] = p.now(sPE)
            ctx[("xr_done", par)] = p.now(sPE)
            ctx[("w8_done", par)] = p.now(sDV)
            for c in range(CPG):
                tok = g * G + c * 128
                mb = 2 + c % 2
                wl = p.op("sync", lambda e, mb=mb, tok=tok: e.dma_start(
                    out=outp[:, mb, :], in_=dout[tok:tok + 128, :]),
                    waits=[attn_store[CPG - 2], attn_store[CPG - 1],
                           ctx.get(("moutp_free", mb))],
                    inc=sLM[mb - 2], dma=True)
                ev = DVE(lambda e, mb=mb, c=c: e.tensor_tensor(
                    outp[:, mb, :], outp[:, mb, :], acc[:, c, :], ALU.add),
                    waits=[wl, ctx[("accw", c, 0)], ctx[("accw", c, 1)]])
                ctx[("acc_free", c)] = ev
                st = st_dma(dout[tok:tok + 128, :], outp[:, mb, :],
                            waits=[ev], sem=sSM[mb - 2])
                ctx[("moutp_free", mb)] = st
                ctx[("w8_free", par, c)] = ev
                yield

        gen = conv_gen()
        for g in range(NGRP):
            gis = g % GPS
            t0 = g * G
            par = g % 2
            chunk_store = [None] * CPG
            attn_store = [None] * CPG

            def pump(n):
                if gen is None:
                    return
                for _ in range(n):
                    try:
                        next(gen)
                    except StopIteration:
                        break

            def drain():
                if gen is None:
                    return
                for _ in gen:
                    pass

            if not pipelined:
                drain()

            if gis == 0 and g > 0:
                gbar = [p.now(sPE), p.now(sDV), p.now(sAC)]
                ev = POOL(lambda e: e.memset(S32[:, :, :], 0.0), waits=gbar)
                for h in range(NH):
                    ctx[("S32", h)] = ev
                ev2 = POOL(lambda e: e.memset(v_fir[:, :, 0:32], 0.0),
                           waits=gbar)
                ctx["vtail"] = ev2
                ev3 = POOL(lambda e: e.memset(tails[:, :, :, :], 0.0),
                           waits=gbar)
                ctx["ctail"] = ev3

            for c in range(CPG):
                tok = t0 + c * 128
                xb = c % 2
                st_ = stat[:, xb, :]
                wx = ld(x_t[:, 0, :], dx[tok:tok + 128, :],
                        waits=w_const + [ctx.get(("x_free", xb))],
                        sem=sXp[xb])
                w_sum = DVE(lambda e, xb=xb, st_=st_: e.reduce_sum(
                    st_[:, 0:1], x_t[:, 0, :], axis=AX.X),
                    waits=[wx, ctx.get(("stat_free", xb))])
                w_sqa = ACT(dump[:, :], x_t[:, 0, 0:512], AF.Square,
                            waits=[wx, ctx.get(("stat_free_a", xb)),
                                   ctx.get(("dump_free", 0))],
                            accum=st_[:, 13:14])
                w_sqb = ACT(dump[:, :], x_t[:, 0, 512:1024], AF.Square,
                            waits=[w_sqa], accum=st_[:, 14:15])
                w_sq = DVE(lambda e, st_=st_: e.tensor_tensor(
                    st_[:, 1:2], st_[:, 13:14], st_[:, 14:15], ALU.add),
                    waits=[w_sqb])
                e1 = DVE(lambda e, st_=st_: e.tensor_scalar(
                    st_[:, 2:3], st_[:, 0:1], 1.0 / H, None, ALU.mult),
                    waits=[w_sum])
                e2 = DVE(lambda e, st_=st_: e.tensor_scalar(
                    st_[:, 3:4], st_[:, 1:2], 1.0 / H, None, ALU.mult),
                    waits=[w_sq])
                e3 = DVE(lambda e, st_=st_: e.tensor_tensor(
                    st_[:, 4:5], st_[:, 2:3], st_[:, 2:3], ALU.mult),
                    waits=[e1])
                e4 = DVE(lambda e, st_=st_: e.tensor_tensor(
                    st_[:, 5:6], st_[:, 3:4], st_[:, 4:5], ALU.subtract),
                    waits=[e2, e3])
                e5 = DVE(lambda e, st_=st_: e.tensor_scalar(
                    st_[:, 6:7], st_[:, 5:6], LN_EPS, None, ALU.add),
                    waits=[e4])
                e6 = ACT(st_[:, 7:8], st_[:, 6:7], AF.Ln, waits=[e5])
                e7 = ACT(st_[:, 8:9], st_[:, 7:8], AF.Exp, waits=[e6],
                         scale=-0.5)
                w_xn = DVE(lambda e, xb=xb, st_=st_: e.tensor_scalar(
                    xn_tok[:, 0, :], x_t[:, 0, :], st_[:, 2:3], st_[:, 8:9],
                    ALU.subtract, ALU.mult),
                    waits=[e7, ctx.get(("xn_free", xb))])
                w_op = DVE(lambda e, xb=xb: e.tensor_add(
                    outp[:, xb, :], x_t[:, 0, :], xn_tok[:, 0, :]),
                    waits=[w_xn, ctx.get(("outp_free", xb))])
                ctx[("x_free", xb)] = w_op
                stv = st_dma(dout[tok:tok + 128, :], outp[:, xb, :],
                             waits=[w_op], sem=sST[xb])
                chunk_store[c] = stv
                ctx[("outp_free", xb)] = stv

                w_sq2a = ACT(dump[:, :], xn_tok[:, 0, 0:512], AF.Square,
                             waits=[w_xn, w_sqb], accum=st_[:, 9:10])
                w_sq2b = ACT(dump[:, :], xn_tok[:, 0, 512:1024], AF.Square,
                             waits=[w_sq2a], accum=st_[:, 15:16])
                ctx[("dump_free", 0)] = w_sq2b
                w_sq2 = DVE(lambda e, st_=st_: e.tensor_tensor(
                    st_[:, 9:10], st_[:, 9:10], st_[:, 15:16], ALU.add),
                    waits=[w_sq2b])
                e8 = DVE(lambda e, st_=st_: e.tensor_scalar(
                    st_[:, 10:11], st_[:, 9:10], 1.0 / H, FFN_EPS,
                    ALU.mult, ALU.add), waits=[w_sq2])
                e9 = ACT(st_[:, 11:12], st_[:, 10:11], AF.Ln, waits=[e8])
                eA = ACT(st_[:, 12:13], st_[:, 11:12], AF.Exp, waits=[e9],
                         scale=-0.5)
                ctx[("stat_free", xb)] = eA
                wxr1 = DVE(lambda e, xb=xb, st_=st_: e.tensor_scalar(
                    xr_tok[:, 0, :], xn_tok[:, 0, :], st_[:, 12:13], None,
                    ALU.mult), waits=[eA, ctx.get(("xr_free", xb))])
                ctx[("stat_free_a", xb)] = wxr1
                wxr = DVE(lambda e, xb=xb: e.tensor_mul(
                    xr_tok[:, 0, :], xr_tok[:, 0, :], c_gmb[:, :]),
                    waits=[wxr1])

                a2f = [ctx.get("arena2_free")] + ctx.get("xr32_rd", [])
                for bk in range(2):
                    tr = None
                    for q4 in range(4):
                        f = bk * 4 + q4
                        tr = PET(pa[bk][:, ts(q4)], xr_tok[:, 0, ts(f)],
                                 waits=[wxr, bankw(f"a{bk}")], f32=True)
                    ev = ACT(xr_fm32.rearrange("p a b -> p (a b)")
                             [:, bk * 512:(bk + 1) * 512],
                             pa[bk][:, :], waits=[tr] + a2f)
                    bankr(f"a{bk}", ev)
                w_xr32 = ctx[("bk", "a1")]
                ctx[("xr_free", xb)] = w_xr32
                w_xr16 = DVE(lambda e, c=c, par=par: e.tensor_copy(
                    xr_fm[:, par, :, c * 128:(c + 1) * 128],
                    xr_fm32[:, :, :]),
                    waits=[w_xr32, ctx.get(("xr_done", par))])
                ctx[("xr16", par, c)] = w_xr16

                wrm = None
                for kf in range(8):
                    wrm = MM(pa[2][0:NE, 0:128], r_wr[:, kf, :],
                             xr_fm32[:, kf, :],
                             waits=[w_xr32, bankw("a2")] + w_pre,
                             start=(kf == 0), stop=(kf == 7))
                ctx["xr32_rd"] = [wrm, w_xr16]
                wse = ACT(sc_fm[:, :], pa[2][0:NE, 0:128], waits=[wrm])
                bankr("a2", wse)
                wst = PET(pa[2][:, 128:128 + NE], sc_fm[:, :],
                          waits=[wse, bankw("a2")], f32=True)
                wsc = ACT(ext[:, 0:NE], pa[2][:, 128:128 + NE],
                          waits=[wst, ctx.get("ext_free_a")])
                bankr("a2", wsc)
                wm1 = DVE(lambda e: e.reduce_max(
                    ext[:, 8:9], ext[:, 0:NE], axis=AX.X), waits=[wsc])
                weq1 = DVE(lambda e: e.tensor_scalar(
                    ext[:, 16:16 + NE], ext[:, 0:NE], ext[:, 8:9], None,
                    ALU.is_equal), waits=[wm1])
                wmsk = DVE(lambda e: e.scalar_tensor_tensor(
                    ext[:, 24:24 + NE], ext[:, 16:16 + NE], -1e30,
                    ext[:, 0:NE], ALU.mult, ALU.add), waits=[weq1])
                wm2 = DVE(lambda e: e.reduce_max(
                    ext[:, 9:10], ext[:, 24:24 + NE], axis=AX.X), waits=[wmsk])
                wd_ = DVE(lambda e: e.tensor_tensor(
                    ext[:, 10:11], ext[:, 8:9], ext[:, 9:10], ALU.subtract),
                    waits=[wm2])
                ww1 = ACT(ext[:, 11:12], ext[:, 10:11], AF.Sigmoid, waits=[wd_])
                ww2 = DVE(lambda e: e.tensor_scalar(
                    ext[:, 12:13], ext[:, 11:12], -1.0, 1.0, ALU.mult,
                    ALU.add), waits=[ww1])
                weq2 = DVE(lambda e: e.tensor_scalar(
                    ext[:, 32:32 + NE], ext[:, 0:NE], ext[:, 9:10], None,
                    ALU.is_equal), waits=[wm2])
                wnq1 = DVE(lambda e: e.tensor_scalar(
                    ext[:, 40:40 + NE], ext[:, 16:16 + NE], -1.0, 1.0,
                    ALU.mult, ALU.add), waits=[weq1])
                weq2b = DVE(lambda e: e.tensor_tensor(
                    ext[:, 32:32 + NE], ext[:, 32:32 + NE], ext[:, 40:40 + NE],
                    ALU.mult), waits=[weq2, wnq1])
                wpa = DVE(lambda e: e.tensor_scalar(
                    ext[:, 48:48 + NE], ext[:, 16:16 + NE], ext[:, 11:12],
                    None, ALU.mult), waits=[ww2])
                ww8 = DVE(lambda e, c=c, par=par: e.scalar_tensor_tensor(
                    w8[:, par, c, :], ext[:, 32:32 + NE], ext[:, 12:13],
                    ext[:, 48:48 + NE], ALU.mult, ALU.add),
                    waits=[weq2b, wpa, ctx.get(("w8_free", par, c)),
                           ctx.get(("w8_done", par))])
                ctx[("w8", par, c)] = ww8
                ctx["ext_free_a"] = ww8

                for bk in range(2):
                    tr = None
                    for q4 in range(4):
                        f = bk * 4 + q4
                        tr = PET(pa[bk][:, ts(q4)], xn_tok[:, 0, ts(f)],
                                 waits=[w_xn, bankw(f"a{bk}")], f32=True)
                    ev = ACT(xn_fm[:, bk * 4:(bk + 1) * 4, c * 128:(c + 1) * 128],
                             pa[bk][:, :], waits=[tr, ctx.get("xnfm_free")])
                    bankr(f"a{bk}", ev)
                ctx[("xn_free", xb)] = ctx[("bk", "a1")]
                ctx[("xnT", c)] = ctx[("bk", "a1")]
                pump(16)

            w_xn_all = [ctx[("xnT", cc)] for cc in range(CPG)]

            ev_silu = {}
            unit_blk = {}

            def load_unit(nm):
                b0 = ABLK[nm]
                nb = 4 if nm == "W1" else 2
                unit_blk[nm] = [aload(b0 + i) for i in range(nb)]

            load_unit("Wq")

            def proj_stream(nm, dst_fm, fir_dst=None):
                ui = {"Wq": 0, "Wk": 1, "Wv": 2}[nm]
                evac_evs = []
                for half in range(2):
                    si, wrdy = unit_blk[nm][half]
                    for fb4 in range(4):
                        fb = half * 4 + fb4
                        pbk = fb % 2
                        wm = None
                        for kf in range(8):
                            wm = MM(pa[pbk][:, :],
                                    aring[si][:, kf, ts(fb4)],
                                    xn_fm[:, kf, :],
                                    waits=w_xn_all + [wrdy, bankw(f"a{pbk}")],
                                    start=(kf == 0), stop=(kf == 7))
                        ev = ACT(raw[:, fb, 4:4 + G], pa[pbk][:, :],
                                 waits=[wm, ctx.get("raw_free"),
                                        ctx.get("ctail")])
                        bankr(f"a{pbk}", ev)
                        evac_evs.append(ev)
                    pump(9)
                ctx[("aring", unit_blk[nm][0][0])] = p.now(sPE)
                ctx[("aring", unit_blk[nm][1][0])] = p.now(sPE)
                wtl = DVE(lambda e, ui=ui: e.tensor_copy(
                    raw[:, :, 1:4], tails[:, ui, :, 1:4]),
                    waits=evac_evs + [ctx.get("ctail"), ctx.get("raw_free")])
                wsv = DVE(lambda e, ui=ui: e.tensor_copy(
                    tails[:, ui, :, 1:4], raw[:, :, 1 + G:4 + G]),
                    waits=evac_evs + [wtl])
                ctx["wsv_" + ("v" if fir_dst is not None else "x")] = wsv
                chain = [None] * 8
                cw = c_cw[nm]
                silu_list = []
                for pr in range(4):
                    for i_tap in range(4):
                        for fb in (2 * pr, 2 * pr + 1):
                            ca = cacc[:, fb % 2, :]
                            win = raw[:, fb, 4 - i_tap: 4 - i_tap + G]
                            ct = 3 - i_tap
                            col = cw[:, fb * 4 + ct: fb * 4 + ct + 1]
                            if i_tap == 0:
                                chain[fb] = DVE(
                                    lambda e, ca=ca, win=win, col=col:
                                    e.tensor_scalar(ca, win, col, None,
                                                    ALU.mult),
                                    waits=[wtl, wsv,
                                           ctx.get(("cacc_free", fb % 2)),
                                           evac_evs[fb]])
                            else:
                                chain[fb] = DVE(
                                    lambda e, ca=ca, win=win, col=col:
                                    e.scalar_tensor_tensor(ca, win, col, ca,
                                                           ALU.mult, ALU.add),
                                    waits=[chain[fb]])
                    for fb in (2 * pr, 2 * pr + 1):
                        ca = cacc[:, fb % 2, :]
                        if fir_dst is None:
                            ev = ACT(dst_fm[:, fb, :], ca, AF.Silu,
                                     waits=[chain[fb],
                                            ctx.get(("qk_free", nm))])
                        else:
                            ev = ACT(fir_dst[:, fb, 32:32 + G], ca, AF.Silu,
                                     waits=[chain[fb], ctx.get("vtail")])
                        ctx[("cacc_free", fb % 2)] = ev
                        ev_silu[(nm, fb)] = ev
                        silu_list.append(ev)
                return silu_list, p.now(sDV)

            sl_q, cd = proj_stream("Wq", q_fm)
            ctx["raw_free"] = cd
            load_unit("Wk")
            pump(2)
            sl_k, cd = proj_stream("Wk", k_fm)
            ctx["raw_free"] = cd
            load_unit("Wv")
            pump(2)
            sl_v, cd = proj_stream("Wv", None, fir_dst=v_fir)
            ctx["raw_free"] = cd
            ctx["last_wsv"] = ctx.get("wsv_v")
            pump(2)

            for which in ("b", "f"):
                rw = r_wb if which == "b" else r_wf
                row = 0 if which == "b" else 1
                wm = None
                for kf in range(8):
                    wm = MM(pa[2][0:NH, 0:G], rw[:, kf, :],
                            xn_fm[:, kf, :],
                            waits=w_xn_all + [bankw("a2")] + w_pre,
                            start=(kf == 0), stop=(kf == 7))
                if which == "b":
                    ev = ACT(bf_fm[:, row, :], pa[2][0:NH, 0:G], AF.Sigmoid,
                             waits=[wm, ctx.get("bffm_free")])
                else:
                    ev = ACT(bf_fm[:, row, :], pa[2][0:NH, 0:G], AF.Sigmoid,
                             waits=[wm, ctx.get("bffm_free")],
                             bias=c_fb[:, 0:1])
                bankr("a2", ev)
                ctx[("bf", row)] = ev
            pump(8)

            wf2mm = []
            for half in range(2):
                w1blk = [aload(6 + half * 2), aload(7 + half * 2)]
                h_evs = []
                for q2 in range(2):
                    si, wrdy = w1blk[q2]
                    for fb4 in range(4):
                        fbg = half * 8 + q2 * 4 + fb4
                        fbr = q2 * 4 + fb4
                        pbk = fbr % 2
                        wm = None
                        for kf in range(8):
                            wm = MM(pa[pbk][:, :],
                                    aring[si][:, kf, ts(fb4)],
                                    xn_fm[:, kf, :],
                                    waits=w_xn_all + [wrdy, bankw(f"a{pbk}")],
                                    start=(kf == 0), stop=(kf == 7))
                        ev = ACT(raw[:, fbr, 4:4 + G], pa[pbk][:, :], AF.Gelu,
                                 waits=[wm, ctx.get("raw_fus_free"),
                                        ctx.get("last_wsv")] +
                                 (sl_v if half == 0 else []),
                                 bias=c_b1[:, fbg:fbg + 1])
                        bankr(f"a{pbk}", ev)
                        h_evs.append(ev)
                    pump(8)
                ctx[("aring", w1blk[0][0])] = p.now(sPE)
                ctx[("aring", w1blk[1][0])] = p.now(sPE)
                wm = None
                for kf in range(8):
                    wm = MM(pa[2][0:32, 0:G], r_w2[:, half * 8 + kf, :],
                            raw[:, kf, 4:4 + G],
                            waits=h_evs + [bankw("a2")] + w_pre,
                            start=(half == 0 and kf == 0),
                            stop=(kf == 7))
                wf2mm.append(wm)
                ctx["raw_fus_free"] = wm
            load_unit("Wo")
            wf2 = DVE(lambda e: e.tensor_scalar(
                fus_fm[:, :], pa[2][0:32, 0:G], c_b2[:, 0:1], None, ALU.add),
                waits=[wf2mm[1], ctx[("xr16", par, CPG - 1)]])
            bankr("a2", wf2)
            w_fus = wf2
            ctx["raw_fus_done"] = p.now(sPE)
            pump(8)

            fir_ready = {}
            fchain = {}
            base = 32
            for pr in range(4):
                for i_tap in range(33):
                    for fb in (2 * pr, 2 * pr + 1):
                        ca = cacc[:, fb % 2, 0:G]
                        if i_tap == 0:
                            cs = c_fs[:, fb * 3 + 2: fb * 3 + 3]
                            win = v_fir[:, fb, base:base + G]
                            fchain[fb] = DVE(
                                lambda e, ca=ca, win=win, cs=cs:
                                e.tensor_scalar(ca, win, cs, None, ALU.mult),
                                waits=[ev_silu[("Wv", fb)],
                                       ctx.get(("cacc_free", fb % 2)),
                                       ctx.get("fir_free"),
                                       ctx.get("raw_fus_done")])
                        elif i_tap == 1:
                            cs = c_fs[:, fb * 3 + 1: fb * 3 + 2]
                            win = v_fir[:, fb, base - 1:base - 1 + G]
                            fchain[fb] = DVE(
                                lambda e, ca=ca, win=win, cs=cs:
                                e.scalar_tensor_tensor(ca, win, cs, ca,
                                                       ALU.mult, ALU.add),
                                waits=[fchain[fb]])
                        elif i_tap == 2:
                            cs = c_fs[:, fb * 3: fb * 3 + 1]
                            win = v_fir[:, fb, base - 2:base - 2 + G]
                            fchain[fb] = DVE(
                                lambda e, ca=ca, win=win, cs=cs, fb=fb:
                                e.scalar_tensor_tensor(fir_s[:, fb, :], win,
                                                       cs, ca, ALU.mult,
                                                       ALU.add),
                                waits=[fchain[fb]])
                            cl = c_fl[:, fb * 31 + 30: fb * 31 + 31]
                            winl = v_fir[:, fb, base:base + G]
                            fchain[fb] = DVE(
                                lambda e, ca=ca, winl=winl, cl=cl:
                                e.tensor_scalar(ca, winl, cl, None, ALU.mult),
                                waits=[fchain[fb]])
                        else:
                            lt = i_tap - 2
                            cl = c_fl[:, fb * 31 + 30 - lt: fb * 31 + 31 - lt]
                            win = v_fir[:, fb, base - lt: base - lt + G]
                            if lt < 30:
                                fchain[fb] = DVE(
                                    lambda e, ca=ca, win=win, cl=cl:
                                    e.scalar_tensor_tensor(ca, win, cl, ca,
                                                           ALU.mult, ALU.add),
                                    waits=[fchain[fb]])
                            else:
                                ev = DVE(
                                    lambda e, ca=ca, win=win, cl=cl, fb=fb:
                                    e.scalar_tensor_tensor(
                                        raw[:, fb, 4:4 + G], win, cl, ca,
                                        ALU.mult, ALU.add),
                                    waits=[fchain[fb]])
                                fchain[fb] = ev
                                ctx[("cacc_free", fb % 2)] = ev
                                fir_ready[fb] = ev
            pump(10)

            w_a1dv = p.now(sDV)
            for c in range(CPG):
                tb1 = PET(pt_a[:, 0:NH], bf_fm[:, 0, c * 128:(c + 1) * 128],
                          waits=[ctx[("bf", 0)], bankw("t0")])
                tb2 = PET(pt_a[:, 128:128 + NH],
                          bf_fm[:, 1, c * 128:(c + 1) * 128],
                          waits=[ctx[("bf", 1)]])
                wbe = ACT(beta_b[:, c, :], pt_a[:, 0:NH],
                          waits=[tb2, ctx.get(("beta_free", c))])
                wfe = ACT(ext[:, 0:NH], pt_a[:, 128:128 + NH],
                          waits=[tb2, w_a1dv, ctx.get("ext_free")])
                bankr("t0", wfe)
                wb3 = PET(pa[2][:, 256:256 + 32],
                          fus_fm[:, c * 128:(c + 1) * 128],
                          waits=[w_fus, bankw("a2")], f32=True)
                wfu = ACT(fus_tok[:, :], pa[2][:, 256:256 + 32],
                          waits=[wb3, ctx.get("fus_tok_free")])
                bankr("a2", wfu)
                ctx[("betaT", c)] = wbe
                s0 = fus_tok[:, 0:32:4]
                s1 = fus_tok[:, 1:32:4]
                s2 = fus_tok[:, 2:32:4]
                s3 = fus_tok[:, 3:32:4]
                wmx = DVE(lambda e, s0=s0, s1=s1: e.tensor_tensor(
                    ext[:, 8:16], s0, s1, ALU.max), waits=[wfu, w_a1dv])
                wmx = DVE(lambda e, s2=s2: e.tensor_tensor(
                    ext[:, 8:16], ext[:, 8:16], s2, ALU.max), waits=[wmx])
                wd0 = DVE(lambda e, s0=s0: e.tensor_tensor(
                    ext[:, 16:24], s0, ext[:, 8:16], ALU.subtract), waits=[wmx])
                wd1 = DVE(lambda e, s1=s1: e.tensor_tensor(
                    ext[:, 24:32], s1, ext[:, 8:16], ALU.subtract), waits=[wmx])
                wd2 = DVE(lambda e, s2=s2: e.tensor_tensor(
                    ext[:, 32:40], s2, ext[:, 8:16], ALU.subtract), waits=[wmx])
                wex = ACT(ext[:, 16:40], ext[:, 16:40], AF.Exp,
                          waits=[wd0, wd1, wd2])
                wsm = DVE(lambda e: e.tensor_tensor(
                    ext[:, 40:48], ext[:, 16:24], ext[:, 24:32], ALU.add),
                    waits=[wex])
                wsm = DVE(lambda e: e.tensor_tensor(
                    ext[:, 40:48], ext[:, 40:48], ext[:, 32:40], ALU.add),
                    waits=[wsm])
                wrc = DVE(lambda e: e.reciprocal(ext[:, 48:56], ext[:, 40:48]),
                          waits=[wsm])
                ws3 = ACT(ext[:, 56:64], s3, AF.Sigmoid, waits=[wfu])
                wfl = DVE(lambda e: e.tensor_scalar(
                    ext[:, 0:NH], ext[:, 0:NH], -0.2, 1.0, ALU.mult, ALU.add),
                    waits=[wfe])
                wpv = DVE(lambda e, c=c: e.tensor_tensor(
                    coef[:, c, 3, :], ext[:, 56:64], ext[:, 0:NH], ALU.mult),
                    waits=[ws3, wfl, ctx.get(("coef_free", c))])
                wq1 = DVE(lambda e, c=c: e.tensor_scalar(
                    ext[:, 56:64], coef[:, c, 3, :], -1.0, 1.0, ALU.mult,
                    ALU.add), waits=[wpv])
                wcw = DVE(lambda e: e.tensor_tensor(
                    ext[:, 48:56], ext[:, 48:56], ext[:, 56:64], ALU.mult),
                    waits=[wq1, wrc])
                for j, sj in enumerate((slice(16, 24), slice(24, 32),
                                        slice(32, 40))):
                    DVE(lambda e, c=c, j=j, sj=sj: e.tensor_tensor(
                        coef[:, c, j, :], ext[:, sj], ext[:, 48:56], ALU.mult),
                        waits=[wcw])
                ctx[("coef", c)] = p.now(sDV)
                ctx["ext_free"] = p.now(sDV)
                ctx["fus_tok_free"] = wd2
                pump(8)
            ctx["bffm_free"] = p.now(sPE)
            ctx["arena2_free"] = p.now(sPE)
            ctx["ext_free_a"] = p.now(sDV)

            L_ = dict(
                p=p, ctx=ctx, MM=MM, PET=PET, ACT=ACT, DVE=DVE,
                bankw=bankw, bankr=bankr,
                q_fm=q_fm, k_fm=k_fm, v_fir=v_fir, fir_s=fir_s, raw=raw,
                fir_ready=fir_ready,
                pa=pa, pt_a=pt_a,
                d_tok=d_tok, d_fm3=d_fm3, d_knt=d_knt, d_kbt=d_kbt,
                d_vbt=d_vbt, d_AAT=d_AAT, d_PP=d_PP, d_T=d_T, d_uw=d_uw,
                d_alT=d_alT, d_unew=d_unew, d_o2=d_o2, d_ft=d_ft,
                o_tmp=o_tmp, o2_fm=o2_fm, dstat=dstat, dump=dump,
                S32=S32, s16t=s16t, beta_b=beta_b, coef=coef,
                c_mslu=c_mslu, c_miu=c_miu, c_id16=c_id16, c_pfb=c_pfb,
                c_nob=c_nob, ev_silu=ev_silu, sPE=sPE, sDV=sDV, sAC=sAC,
                sLA=sLA, sSE=sSE, ld=ld, st_dma=st_dma, dout=dout,
                chunk_store=chunk_store, attn_store=attn_store,
                t0=t0, aring=aring, unit_blk=unit_blk, outp=outp,
            )
            for c in range(CPG):
                for h in range(NH):
                    emit_delta(p, ctx, L_, c, h)
                emit_epilogue(p, ctx, L_, c, g)
                pump(6)
            pump(6)
            ctx[("aring", unit_blk["Wo"][0][0])] = p.now(sPE)
            ctx[("aring", unit_blk["Wo"][1][0])] = p.now(sPE)

            ev = DVE(lambda e: e.tensor_copy(
                v_fir[:, :, 0:32], v_fir[:, :, G:G + 32]),
                waits=sl_v + [p.now(sPE)])
            ctx["vtail"] = ev
            ctx["ctail"] = p.now(sDV)
            ctx[("qk_free", "Wq")] = p.now(sPE)
            ctx[("qk_free", "Wk")] = p.now(sPE)
            ctx["xnfm_free"] = p.now(sPE)
            ctx["fir_free"] = p.now(sPE)
            ctx["raw_free"] = p.now(sPE)
            ctx[("cacc_free", 0)] = p.now(sDV)
            ctx[("cacc_free", 1)] = p.now(sDV)

            drain()
            gen = moe_gen(g, list(attn_store))
            if g == NGRP - 1:
                drain()
                gen = None

        allsems = ([sCT, sWD, sPE, sDV, sAC, sPO] + sXp + sLA + sLM +
                   sST + sSE + sSM + sARs + sMRs + sCS)
        p.op("sync", lambda e: e.nop(), waits=[p.now(s) for s in allsems])

        sems = [stack.enter_context(nc.semaphore(n)) for n in p.names]
        with nc.Block() as block:
            block.tensor(lambda e: p.replay("pe", e, sems))
            block.vector(lambda e: p.replay("dve", e, sems))
            block.scalar(lambda e: p.replay("act", e, sems))
            block.gpsimd(lambda e: p.replay("pool", e, sems))
            block.sync(lambda e: p.replay("sync", e, sems))
    return nc


def emit_delta(p, ctx, L_, c, h):
    MM, PET, ACT, DVE = L_["MM"], L_["PET"], L_["ACT"], L_["DVE"]
    bankw, bankr = L_["bankw"], L_["bankr"]
    q_fm, k_fm, v_fir = L_["q_fm"], L_["k_fm"], L_["v_fir"]
    fir_s, raw, fir_ready = L_["fir_s"], L_["raw"], L_["fir_ready"]
    pa, pt_a = L_["pa"], L_["pt_a"]
    d_tok, d_fm3 = L_["d_tok"], L_["d_fm3"]
    d_knt, d_kbt, d_vbt = L_["d_knt"], L_["d_kbt"], L_["d_vbt"]
    d_AAT, d_PP, d_T = L_["d_AAT"], L_["d_PP"], L_["d_T"]
    d_uw, d_alT, d_unew, d_o2 = L_["d_uw"], L_["d_alT"], L_["d_unew"], L_["d_o2"]
    d_ft, o_tmp, o2_fm = L_["d_ft"], L_["o_tmp"], L_["o2_fm"]
    dstat, dump = L_["dstat"], L_["dump"]
    S32, s16t = L_["S32"], L_["s16t"]
    beta_b, coef = L_["beta_b"], L_["coef"]
    c_mslu, c_miu, c_id16, c_pfb, c_nob = (L_["c_mslu"], L_["c_miu"],
                                           L_["c_id16"], L_["c_pfb"],
                                           L_["c_nob"])
    ev_silu = L_["ev_silu"]
    sPE, sDV = L_["sPE"], L_["sDV"]
    tw = c * 128
    st = dstat[:, :]
    free = ctx.get("dfree")

    t1 = None
    for j, src in enumerate((q_fm[:, h, tw:tw + 128], k_fm[:, h, tw:tw + 128],
                             v_fir[:, h, 32 + tw:32 + tw + 128])):
        t1 = PET(pt_a[:, ts(j)], src,
                 waits=[ev_silu[("Wq", h)], ev_silu[("Wk", h)],
                        ev_silu[("Wv", h)], bankw("t0"), free])
    e1 = ACT(d_tok[:, :, :].rearrange("p a b -> p (a b)"),
             pt_a[:, 0:384], waits=[t1, free])
    bankr("t0", e1)
    sq = ACT(dump[:, 0:128], d_tok[:, 0, :], AF.Square, waits=[e1],
             accum=st[:, 0:1])
    sk = ACT(dump[:, 128:256], d_tok[:, 1, :], AF.Square, waits=[e1],
             accum=st[:, 1:2])
    ee = DVE(lambda e: e.tensor_scalar(st[:, 2:4], st[:, 0:2], L2_EPS, None,
                                       ALU.add), waits=[sq, sk])
    el = ACT(st[:, 4:6], st[:, 2:4], AF.Ln, waits=[ee])
    er = ACT(st[:, 6:8], st[:, 4:6], AF.Exp, waits=[el], scale=-0.5)
    bcol = beta_b[:, c, h:h + 1]
    ebeta = ctx[("betaT", c)]
    eq = DVE(lambda e: e.tensor_scalar(d_tok[:, 0, :], d_tok[:, 0, :],
                                       st[:, 6:7], None, ALU.mult),
             waits=[er])
    ek = DVE(lambda e: e.tensor_scalar(d_knt[:, :], d_tok[:, 1, :],
                                       st[:, 7:8], None, ALU.mult),
             waits=[er, free])
    ekb = DVE(lambda e: e.tensor_scalar(d_kbt[:, :], d_knt[:, :],
                                        bcol, None, ALU.mult),
              waits=[ek, ebeta, free])
    evb = DVE(lambda e: e.tensor_scalar(d_vbt[:, :], d_tok[:, 2, :],
                                        bcol, None, ALU.mult),
              waits=[e1, ebeta, free])
    t2 = None
    for j, src in enumerate((d_tok[:, 0, :], d_knt[:, :], d_kbt[:, :])):
        t2 = PET(pt_a[:, ts(j)], src, waits=[eq, ek, ekb, bankw("t0")])
    e2 = ACT(d_fm3[:, :, :].rearrange("p a b -> p (a b)"),
             pt_a[:, 0:384], waits=[t2, free])
    bankr("t0", e2)
    qn_f, kn_f, kb_f = (d_fm3[:, 0, :], d_fm3[:, 1, :], d_fm3[:, 2, :])
    a1 = MM(pa[0][:, 0:128], kb_f, kn_f, waits=[e2, bankw("a0")])
    a2 = MM(pa[0][:, 128:256], kn_f, kb_f, waits=[e2])
    eA = DVE(lambda e: e.tensor_mul(
        d_AAT[:, :, :].rearrange("p a b -> p (a b)"),
        pa[0][:, 0:256], c_mslu[:, :, :].rearrange("p a b -> p (a b)")),
        waits=[a2, free])
    bankr("a0", eA)
    prevP, prevT = d_AAT[:, 0, :], d_AAT[:, 1, :]
    prev_ev = eA
    for lvl in range(5):
        m1 = MM(pa[0][:, 0:128], prevT, prevP, waits=[prev_ev, bankw("a0")])
        m2 = MM(pa[0][:, 128:256], prevP, prevT, waits=[prev_ev])
        pev = ACT(d_PP[:, lvl, :, :].rearrange("p a b -> p (a b)"),
                  pa[0][:, 0:256], waits=[m2, free])
        bankr("a0", pev)
        prevP = d_PP[:, lvl, 0, :]
        prevT = d_PP[:, lvl, 1, :]
        prev_ev = pev
    m64 = MM(pa[0][:, 256:384], d_PP[:, 4, 0, :], d_PP[:, 4, 1, :],
             waits=[prev_ev, bankw("a0")])
    eT = DVE(lambda e: e.tensor_add(d_T[:, 0, :], pa[0][:, 256:384],
                                    c_id16[:, :]), waits=[m64, free])
    bankr("a0", eT)
    cur = 0
    for kk in [4, 3, 2, 1, 0, -1]:
        lhsT = d_AAT[:, 0, :] if kk < 0 else d_PP[:, kk, 0, :]
        mp = MM(pa[0][:, 384:512], lhsT, d_T[:, cur, :],
                waits=[eT, bankw("a0")])
        eT = DVE(lambda e, cur=cur: e.tensor_add(
            d_T[:, 1 - cur, :], pa[0][:, 384:512], d_T[:, cur, :]),
            waits=[mp])
        bankr("a0", eT)
        cur = 1 - cur
    Tf = d_T[:, cur, :]
    cs16 = ACT(s16t[:, :], S32[:, h, :],
               waits=[ctx[("S32", h)], ctx.get("s16_free")])
    mu = MM(pa[1][:, 0:128], Tf, d_vbt[:, :],
            waits=[eT, evb, bankw("a1")])
    mw = MM(pa[1][:, 128:256], d_kbt[:, :], Tf, waits=[eT, ekb])
    euw = ACT(d_uw[:, :, :].rearrange("p a b -> p (a b)"),
              pa[1][:, 0:256], waits=[mw, free])
    bankr("a1", euw)
    mal = MM(pa[1][:, 256:384], kn_f, qn_f, waits=[e2, bankw("a1")])
    eal = DVE(lambda e: e.tensor_mul(d_alT[:, :], pa[1][:, 256:384],
                                     c_miu[:, :]), waits=[mal, free])
    bankr("a1", eal)
    mws = MM(pa[1][:, 384:512], d_uw[:, 1, :], s16t[:, :],
             waits=[euw, cs16, bankw("a1")])
    eun = DVE(lambda e: e.tensor_tensor(d_unew[:, :], d_uw[:, 0, :],
                                        pa[1][:, 384:512], ALU.subtract),
              waits=[mws, free])
    bankr("a1", eun)
    mo1 = MM(pa[2][:, 0:128], qn_f, s16t[:, :],
             waits=[e2, cs16, bankw("a2")], start=True, stop=False)
    mo = MM(pa[2][:, 0:128], d_alT[:, :], d_unew[:, :],
            waits=[eal, eun], start=False, stop=True)
    ctx["s16_free"] = mo
    mds = MM(pa[1][:, 0:128], d_knt[:, :], d_unew[:, :],
             waits=[ek, eun, bankw("a1")])
    es1 = DVE(lambda e: e.tensor_tensor(S32[:, h, :], S32[:, h, :],
                                        pa[1][:, 0:128], ALU.add),
              waits=[mds, ctx[("S32", h)]])
    ctx[("S32", h)] = es1
    bankr("a1", es1)
    tf = PET(pt_a[:, 0:128], fir_s[:, h, tw:tw + 128],
             waits=[fir_ready[h], bankw("t0")])
    tf = PET(pt_a[:, 128:256], raw[:, h, 4 + tw:4 + tw + 128],
             waits=[fir_ready[h]])
    eft = ACT(d_ft[:, :, :].rearrange("p a b -> p (a b)"),
              pt_a[:, 0:256], waits=[tf, free])
    bankr("t0", eft)
    wcf = ctx[("coef", c)]
    cc0 = coef[:, c, 0, h:h + 1]
    cc1 = coef[:, c, 1, h:h + 1]
    cc2 = coef[:, c, 2, h:h + 1]
    ccp = coef[:, c, 3, h:h + 1]
    eo = DVE(lambda e: e.tensor_scalar(o_tmp[:, :], d_ft[:, 0, :],
                                       cc0, None, ALU.mult),
             waits=[eft, wcf, free])
    eo = DVE(lambda e: e.scalar_tensor_tensor(
        o_tmp[:, :], d_ft[:, 1, :], cc1, o_tmp[:, :],
        ALU.mult, ALU.add), waits=[eo])
    eo = DVE(lambda e: e.scalar_tensor_tensor(
        o_tmp[:, :], pa[2][:, 0:128], cc2, o_tmp[:, :],
        ALU.mult, ALU.add), waits=[eo, mo])
    eo = DVE(lambda e: e.scalar_tensor_tensor(
        o_tmp[:, :], d_tok[:, 2, :], ccp, o_tmp[:, :],
        ALU.mult, ALU.add), waits=[eo])
    bankr("a2", eo)
    sq1 = ACT(dump[:, 256:384], o_tmp[:, :], AF.Square, waits=[eo],
              accum=st[:, 8:9])
    em = DVE(lambda e: e.tensor_scalar(st[:, 9:10], st[:, 8:9], 1.0 / 128,
                                       RMS_EPS, ALU.mult, ALU.add),
             waits=[sq1])
    el1 = ACT(st[:, 10:11], st[:, 9:10], AF.Ln, waits=[em])
    er1 = ACT(st[:, 11:12], st[:, 10:11], AF.Exp, waits=[el1], scale=-0.5)
    eo1 = DVE(lambda e: e.scalar_tensor_tensor(
        o_tmp[:, :], o_tmp[:, :], st[:, 11:12], c_pfb[:, :],
        ALU.mult, ALU.mult), waits=[er1, free])
    sq2 = ACT(dump[:, 384:512], o_tmp[:, :], AF.Square, waits=[eo1],
              accum=st[:, 12:13])
    em2 = DVE(lambda e: e.tensor_scalar(st[:, 13:14], st[:, 12:13], 1.0 / 128,
                                        RMS_EPS, ALU.mult, ALU.add),
              waits=[sq2])
    el2 = ACT(st[:, 14:15], st[:, 13:14], AF.Ln, waits=[em2])
    er2 = ACT(st[:, 15:16], st[:, 14:15], AF.Exp, waits=[el2], scale=-0.5)
    eo2 = DVE(lambda e: e.scalar_tensor_tensor(
        d_o2[:, :], o_tmp[:, :], st[:, 15:16], c_nob[:, :],
        ALU.mult, ALU.mult), waits=[er2, free])
    to2 = PET(pt_a[:, 0:128], d_o2[:, :], waits=[eo2, bankw("t0")])
    ef = ACT(o2_fm[:, h, :], pt_a[:, 0:128],
             waits=[to2, ctx.get(("o2_free", h))])
    bankr("t0", ef)
    ctx[("o2", h)] = ef
    ctx["dfree"] = ef


def emit_epilogue(p, ctx, L_, c, g):
    sPE, sDV = L_["sPE"], L_["sDV"]
    MM, DVE = L_["MM"], L_["DVE"]
    bankw, bankr = L_["bankw"], L_["bankr"]
    pa, o2_fm, outp = L_["pa"], L_["o2_fm"], L_["outp"]
    aring, unit_blk, dout = L_["aring"], L_["unit_blk"], L_["dout"]
    st_dma = L_["st_dma"]
    chunk_store, attn_store = L_["chunk_store"], L_["attn_store"]
    t0 = L_["t0"]
    tok = t0 + c * 128
    xb = c % 2
    o2evs = [ctx[("o2", h)] for h in range(NH)]
    wl = p.op("sync", lambda e, xb=xb, tok=tok: e.dma_start(
        out=outp[:, xb, :], in_=dout[tok:tok + 128, :]),
        waits=[chunk_store[CPG - 2], chunk_store[CPG - 1],
               ctx.get(("outp_free", xb))],
        inc=L_["sLA"][xb], dma=True)
    adds = []
    for nh in range(2):
        si, wrdy = unit_blk["Wo"][nh]
        bk = nh
        wm = None
        for kf in range(8):
            wm = MM(pa[bk][:, :], o2_fm[:, kf, :],
                    aring[si][:, kf, :],
                    waits=o2evs + [wrdy, bankw(f"a{bk}")],
                    start=(kf == 0), stop=(kf == 7))
        ev = DVE(lambda e, xb=xb, nh=nh, bk=bk: e.tensor_tensor(
            outp[:, xb, nh * 512:(nh + 1) * 512],
            outp[:, xb, nh * 512:(nh + 1) * 512],
            pa[bk][:, :], ALU.add), waits=[wm, wl])
        bankr(f"a{bk}", ev)
        adds.append(ev)
    st = st_dma(dout[tok:tok + 128, :], outp[:, xb, :], waits=adds,
                sem=L_["sSE"][xb])
    attn_store[c] = st
    ctx[("outp_free", xb)] = st
    for h in range(NH):
        ctx[("o2_free", h)] = p.now(sPE)
    ctx[("beta_free", c)] = p.now(sDV)
    ctx[("coef_free", c)] = p.now(sDV)


def make_inputs(i):
    base = {
        "Wq": np.ascontiguousarray(i["L0_attn_Wq"], np.float32),
        "Wk": np.ascontiguousarray(i["L0_attn_Wk"], np.float32),
        "Wv": np.ascontiguousarray(i["L0_attn_Wv"], np.float32),
        "Wo": np.ascontiguousarray(i["L0_attn_Wo"], np.float32),
        "fus_W1": np.ascontiguousarray(i["L0_attn_fus_W1"], np.float32),
        "fus_W2": np.ascontiguousarray(i["L0_attn_fus_W2"], np.float32),
        "Wb": np.ascontiguousarray(i["L0_attn_Wb"], np.float32),
        "floor_W": np.ascontiguousarray(i["L0_attn_floor_W"], np.float32),
        "w_routerT": np.ascontiguousarray(
            np.asarray(i["L0_ffn_w_router"]).T, np.float32),
        "w_upT": np.ascontiguousarray(
            np.asarray(i["L0_ffn_w_up"]).transpose(0, 2, 1), np.float32),
        "w_downT": np.ascontiguousarray(
            np.asarray(i["L0_ffn_w_down"]).transpose(0, 2, 1), np.float32),
        "conv_q_pt": np.ascontiguousarray(
            np.asarray(i["L0_attn_conv_q_w"]).reshape(8, 128, 4)
            .transpose(1, 0, 2).reshape(128, 32), np.float32),
        "conv_k_pt": np.ascontiguousarray(
            np.asarray(i["L0_attn_conv_k_w"]).reshape(8, 128, 4)
            .transpose(1, 0, 2).reshape(128, 32), np.float32),
        "conv_v_pt": np.ascontiguousarray(
            np.asarray(i["L0_attn_conv_v_w"]).reshape(8, 128, 4)
            .transpose(1, 0, 2).reshape(128, 32), np.float32),
        "fir_s_pt": np.ascontiguousarray(
            np.asarray(i["L0_attn_fir_short_w"]).transpose(1, 0, 2)
            .reshape(128, 24), np.float32),
        "fir_l_pt": np.ascontiguousarray(
            np.asarray(i["L0_attn_fir_long_w"]).transpose(1, 0, 2)
            .reshape(128, 248), np.float32),
        "fus_b1_pt": np.ascontiguousarray(
            np.asarray(i["L0_attn_fus_b1"]).reshape(16, 128).T, np.float32),
        "fus_b2_c": np.ascontiguousarray(
            np.asarray(i["L0_attn_fus_b2"]).reshape(32, 1), np.float32),
        "floor_b_c": np.ascontiguousarray(
            np.asarray(i["L0_attn_floor_b"]).reshape(NH, 1), np.float32),
        "norm_pf_r": np.ascontiguousarray(
            np.asarray(i["L0_attn_norm_pf_w"]).reshape(1, 128), np.float32),
        "norm_o_r": np.ascontiguousarray(
            np.asarray(i["L0_attn_norm_o_w"]).reshape(1, 128), np.float32),
        "gamma_r": np.ascontiguousarray(
            np.asarray(i["L0_ffn_norm_scale"]).reshape(1, H), np.float32),
    }
    hs = np.ascontiguousarray(np.asarray(i["L0_attn_hidden_states"]),
                              dtype=np.float32)
    in_maps = []
    for c in range(NCORES):
        m = dict(base)
        m["x"] = np.ascontiguousarray(
            hs[c * B_CORE:(c + 1) * B_CORE].reshape(TOK, H))
        in_maps.append(m)
    return in_maps


def kernel(**inputs):
    nc = build()
    in_maps = make_inputs(inputs)
    res = run_bass_kernel_spmd(nc, in_maps, core_ids=list(range(NCORES)))
    out = np.concatenate([r["out"].reshape(B_CORE, L, H)
                          for r in res.results], axis=0)
    return out


# version 10
# speedup vs baseline: 1.0136x; 1.0136x over previous
import numpy as np
import contextlib
import concourse.bass as bass
import concourse.mybir as mybir
from concourse.bass_utils import run_bass_kernel_spmd

F32 = mybir.dt.float32
BF16 = mybir.dt.bfloat16
AF = mybir.ActivationFunctionType
ALU = mybir.AluOpType
AX = mybir.AxisListType

NCORES = 8
B_CORE = 4
L = 2048
H = 1024
NH = 8
HID2 = 2048
NE = 8
EHID = 2048
G = 512
NGRP = B_CORE * L // G
GPS = L // G
CPG = G // 128
TOK = B_CORE * L
LN_EPS = 1e-5
RMS_EPS = 1e-5
FFN_EPS = 1e-6
L2_EPS = 1e-6
NBLK = 12 + NE * 12


class Prog:
    def __init__(self):
        self.ops = []
        self.names = []
        self.counts = []

    def sem(self, name):
        i = len(self.names)
        self.names.append(name)
        self.counts.append(0)
        return i

    def now(self, s):
        return (s, self.counts[s])

    def op(self, eng, fn, waits=(), inc=None, dma=False):
        incs = []
        if inc is not None:
            for s in ([inc] if isinstance(inc, int) else inc):
                d = 16 if dma else 1
                self.counts[s] += d
                incs.append((s, d))
        self.ops.append((eng, fn, [w for w in waits if w is not None], incs))
        if incs:
            return (incs[0][0], self.counts[incs[0][0]])
        return None

    def replay(self, eng_name, eng, sems):
        hw = {}
        for (e, fn, waits, incs) in self.ops:
            if e != eng_name:
                continue
            best = {}
            for (s, v) in waits:
                if v > 0 and v > hw.get(s, 0) and v > best.get(s, 0):
                    best[s] = v
            for s, v in best.items():
                eng.wait_ge(sems[s], v)
                hw[s] = v
            inst = fn(eng)
            for (s, d) in incs:
                inst.then_inc(sems[s], d)


def ts(i, n=128):
    return slice(i * n, (i + 1) * n)


def build(pipelined=True):
    nc = bass.Bass("TRN2", target_bir_lowering=False)
    p = Prog()

    dx = nc.dram_tensor("x", [TOK, H], F32, kind="ExternalInput").ap()
    dout = nc.dram_tensor("out", [TOK, H], F32, kind="ExternalOutput").ap()
    dW = {}
    for nm, shp in [("Wq", [H, H]), ("Wk", [H, H]), ("Wv", [H, H]),
                    ("Wo", [H, H]), ("fus_W1", [H, HID2]),
                    ("fus_W2", [HID2, 32]), ("Wb", [H, NH]),
                    ("floor_W", [H, NH]), ("w_routerT", [H, NE]),
                    ("w_upT", [NE, H, 2 * EHID]), ("w_downT", [NE, EHID, H]),
                    ("conv_q_pt", [128, 32]), ("conv_k_pt", [128, 32]),
                    ("conv_v_pt", [128, 32]), ("fir_s_pt", [128, 24]),
                    ("fir_l_pt", [128, 248]), ("fus_b1_pt", [128, 16]),
                    ("fus_b2_c", [32, 1]), ("floor_b_c", [NH, 1]),
                    ("norm_pf_r", [1, 128]), ("norm_o_r", [1, 128]),
                    ("gamma_r", [1, H])]:
        dW[nm] = nc.dram_tensor(nm, shp, F32, kind="ExternalInput").ap()
    scr = nc.dram_tensor("scr", [NBLK, 128, 8, G], BF16, kind="Internal").ap()

    sCT = p.sem("const_dma")
    sXp = [p.sem("x_dma0"), p.sem("x_dma1")]
    sLA = [p.sem("eload0"), p.sem("eload1")]
    sLM = [p.sem("mload0"), p.sem("mload1")]
    sST = [p.sem("a1st0"), p.sem("a1st1")]
    sSE = [p.sem("epist0"), p.sem("epist1")]
    sSM = [p.sem("moest0"), p.sem("moest1")]
    sARs = [p.sem("aring0"), p.sem("aring1")]
    sMRs = [p.sem("mring0"), p.sem("mring1"), p.sem("mring2")]
    sCS = [p.sem(f"convst{i}") for i in range(5)]
    sWD = p.sem("wd_dma")
    sPE = p.sem("pe")
    sDV = p.sem("dve")
    sAC = p.sem("act")
    sPO = p.sem("pool")

    stack = contextlib.ExitStack()
    sb = lambda name, shape, dt: stack.enter_context(nc.sbuf_tensor(name, shape, dt))
    psa = lambda name, shape, dt: stack.enter_context(nc.psum_tensor(name, shape, dt))

    with stack:
        c_cw = {"Wq": sb("cwq", [128, 32], F32), "Wk": sb("cwk", [128, 32], F32),
                "Wv": sb("cwv", [128, 32], F32)}
        c_fs = sb("firs", [128, 24], F32)
        c_fl = sb("firl", [128, 248], F32)
        c_b1 = sb("fusb1", [128, 16], F32)
        c_b2 = sb("fusb2", [32, 1], F32)
        c_fb = sb("floorb", [NH, 1], F32)
        c_ones = sb("onesrow", [1, 128], F32)
        c_id16 = sb("id16", [128, 128], BF16)
        c_idf = sb("idf", [128, 128], F32)
        c_mslu = sb("mslu", [128, 2, 128], F32)
        c_miu = sb("incl_up", [128, 128], F32)
        c_pfb = sb("pf_b", [128, 128], F32)
        c_nob = sb("no_b", [128, 128], F32)
        c_gmb = sb("gm_b", [128, H], F32)
        r_wb = sb("r_wb", [128, 8, NH], BF16)
        r_wf = sb("r_wf", [128, 8, NH], BF16)
        r_wr = sb("r_wr", [128, 8, NE], F32)
        r_w2 = sb("r_w2", [128, 16, 32], BF16)

        x_t = sb("x_t", [128, 1, H], F32)
        xn_tok = sb("xn_tok", [128, 1, H], F32)
        xr_tok = sb("xr_tok", [128, 1, H], F32)
        dump = sb("dump", [128, 512], F32)
        stat = sb("stat", [128, 2, 16], F32)
        ext = sb("ext", [128, 64], F32)
        outp = sb("outp", [128, 4, H], F32)
        xn_fm = sb("xn_fm", [128, 8, G], BF16)
        xr_fm = sb("xr_fm", [128, 2, 8, G], BF16)
        w8 = sb("w8", [128, 2, CPG, NE], F32)
        beta_b = sb("beta_b", [128, CPG, NH], F32)
        coef = sb("coef", [128, CPG, 4, NH], F32)
        sc_fm = sb("sc_fm", [NE, 128], F32)
        bf_fm = sb("bf_fm", [NH, 2, G], BF16)
        fus_tok = sb("fus_tok", [128, 32], F32)
        arena2 = sb("arena2", [128, 4096], mybir.dt.uint8)
        xr_fm32 = arena2.ap()[:, :].bitcast(F32).rearrange(
            "p (a b) -> p a b", a=8)
        fus_fm = arena2.ap()[0:32, 0:2048].bitcast(F32)

        raw = sb("raw", [128, 8, 4 + G], BF16)
        tails = sb("tails", [128, 3, 8, 4], BF16)
        cacc = sb("cacc", [128, 2, G], F32)
        q_fm = sb("q_fm", [128, 8, G], BF16)
        k_fm = sb("k_fm", [128, 8, G], BF16)
        v_fir = sb("v_fir", [128, 8, 32 + G], BF16)
        fir_s = sb("fir_s", [128, 8, G], BF16)

        aring = [sb(f"aring{i}", [128, 8, G], BF16) for i in range(2)]
        mring = [sb(f"mring{i}", [128, 8, G], BF16) for i in range(3)]

        S32 = sb("S32", [128, NH, 128], F32)
        s16t = sb("s16t", [128, 128], BF16)
        dstat = sb("dstat", [128, 16], F32)
        d_tok = sb("d_tok", [128, 3, 128], BF16)
        d_fm3 = sb("d_fm3", [128, 3, 128], BF16)
        d_knt = sb("d_knt", [128, 128], BF16)
        d_kbt = sb("d_kbt", [128, 128], BF16)
        d_vbt = sb("d_vbt", [128, 128], BF16)
        d_AAT = sb("d_AAT", [128, 2, 128], BF16)
        d_PP = sb("d_PP", [128, 5, 2, 128], BF16)
        d_T = sb("d_T", [128, 2, 128], BF16)
        d_uw = sb("d_uw", [128, 2, 128], BF16)
        d_alT = sb("d_alT", [128, 128], BF16)
        d_unew = sb("d_unew", [128, 128], BF16)
        d_o2 = sb("d_o2", [128, 128], BF16)
        d_ft = sb("d_ft", [128, 2, 128], BF16)
        o_tmp = sb("o_tmp", [128, 128], F32)
        o2_fm = sb("o2_fm", [128, 8, 128], BF16)

        hid = sb("hid", [128, 16, G], BF16)
        acc = sb("acc", [128, CPG, H], F32)
        sg_ = sb("sg", [128, 2, G], BF16)

        pa = [psa(f"pa{i}", [128, 512], F32) for i in range(3)]
        pt_a = psa("pta", [128, 1024], BF16)
        pm = [psa(f"pm{i}", [128, 512], F32) for i in range(4)]

        ctx = {}

        def bankw(key):
            return ctx.get(("bk", key))

        def bankr(key, ev):
            ctx[("bk", key)] = ev

        def ld(dst, src, waits=(), sem=None):
            return p.op("sync", lambda e, d=dst, s=src: e.dma_start(out=d, in_=s),
                        waits=waits, inc=(sCT if sem is None else sem), dma=True)

        def ldw(dst, src, waits=(), sem=None):
            return p.op("pool", lambda e, d=dst, s=src: e.dma_start(out=d, in_=s),
                        waits=waits, inc=(sWD if sem is None else sem), dma=True)

        def st_dma(dst, src, waits=(), sem=None):
            return p.op("sync", lambda e, d=dst, s=src: e.dma_start(out=d, in_=s),
                        waits=waits, inc=(sCT if sem is None else sem), dma=True)

        ld(c_cw["Wq"][:, :], dW["conv_q_pt"])
        ld(c_cw["Wk"][:, :], dW["conv_k_pt"])
        ld(c_cw["Wv"][:, :], dW["conv_v_pt"])
        ld(c_fs[:, :], dW["fir_s_pt"])
        ld(c_fl[:, :], dW["fir_l_pt"])
        ld(c_b1[:, :], dW["fus_b1_pt"])
        ld(c_b2[:, :], dW["fus_b2_c"])
        ld(c_fb[:, :], dW["floor_b_c"])
        ld(dump[0:1, 0:128], dW["norm_pf_r"])
        ld(dump[0:1, 128:256], dW["norm_o_r"])
        ld(cacc[0:1, :, :].rearrange("p a b -> p (a b)"), dW["gamma_r"])
        ld(r_wr[:, :, :], dW["w_routerT"].rearrange("(f p) e -> p f e", p=128))
        ldw(r_wb[:, :, :], dW["Wb"].rearrange("(f p) e -> p f e", p=128))
        ldw(r_wf[:, :, :], dW["floor_W"].rearrange("(f p) e -> p f e", p=128))
        ldw(r_w2[:, :, :], dW["fus_W2"].rearrange("(f p) e -> p f e", p=128))
        w_pre = [p.now(sCT), p.now(sWD)]

        def POOL(fn, waits=()):
            return p.op("pool", fn, waits=list(waits) + [p.now(sPO)], inc=sPO)

        POOL(lambda e: e.memset(c_id16[:, :], 0.0))
        POOL(lambda e: e.affine_select(
            out=c_id16[:, :], in_=c_id16[:, :], compare_op=ALU.not_equal,
            fill=1.0, base=0, pattern=[[-1, 128]], channel_multiplier=1))
        POOL(lambda e: e.memset(c_idf[:, :], 0.0))
        POOL(lambda e: e.affine_select(
            out=c_idf[:, :], in_=c_idf[:, :], compare_op=ALU.not_equal,
            fill=1.0, base=0, pattern=[[-1, 128]], channel_multiplier=1))
        POOL(lambda e: e.memset(c_mslu[:, :, :], 0.0))
        POOL(lambda e: e.affine_select(
            out=c_mslu[:, 0, :], in_=c_mslu[:, 0, :], compare_op=ALU.is_ge,
            fill=-1.0, base=0, pattern=[[1, 128]], channel_multiplier=-1))
        POOL(lambda e: e.affine_select(
            out=c_mslu[:, 1, :], in_=c_mslu[:, 1, :], compare_op=ALU.is_ge,
            fill=-1.0, base=0, pattern=[[-1, 128]], channel_multiplier=1))
        POOL(lambda e: e.memset(c_miu[:, :], 0.0))
        POOL(lambda e: e.affine_select(
            out=c_miu[:, :], in_=c_miu[:, :], compare_op=ALU.is_gt,
            fill=1.0, base=0, pattern=[[-1, 128]], channel_multiplier=1))
        POOL(lambda e: e.memset(c_ones[:, :], 1.0))
        POOL(lambda e: e.memset(v_fir[:, :, :], 0.0))
        POOL(lambda e: e.memset(raw[:, :, :], 0.0))
        POOL(lambda e: e.memset(tails[:, :, :, :], 0.0))
        w_msk = POOL(lambda e: e.memset(S32[:, :, :], 0.0))
        for h in range(NH):
            ctx[("S32", h)] = w_msk

        def MM(out_ap, lhsT, rhs, waits=(), start=True, stop=True):
            return p.op("pe", lambda e: e.matmul(out_ap, lhsT, rhs,
                                                 start=start, stop=stop,
                                                 skip_group_check=True),
                        waits=waits, inc=sPE if stop else None)

        def PET(out_ap, in_ap, waits=(), f32=False):
            kp = in_ap.shape[0]
            ident = (c_idf if f32 else c_id16)[0:kp, 0:kp]
            return p.op("pe", lambda e: e.transpose(out_ap, in_ap, ident),
                        waits=waits, inc=sPE)

        def ACT(out_ap, in_ap, func=AF.Copy, waits=(), bias=0.0, scale=1.0,
                accum=None):
            if accum is not None:
                return p.op("act", lambda e: e.activation(
                    out_ap, in_ap, func, bias=bias, scale=scale,
                    accum_out=accum), waits=waits, inc=sAC)
            return p.op("act", lambda e: e.activation(
                out_ap, in_ap, func, bias=bias, scale=scale),
                waits=waits, inc=sAC)

        def DVE(fn, waits=()):
            return p.op("dve", fn, waits=waits, inc=sDV)

        def bcast(dst, src_row, n):
            for j in range(0, n, 512):
                w = min(512, n - j)
                wm = MM(pm[0][:, 0:w], c_ones[:, :], src_row[:, j:j + w],
                        waits=w_pre + [w_msk, bankw("m0")])
                ev = DVE(lambda e, j=j, w=w, d=dst: e.tensor_copy(
                    d[:, j:j + w], pm[0][:, 0:w]), waits=[wm])
                bankr("m0", ev)
        bcast(c_pfb, dump[0:1, 0:128], 128)
        bcast(c_nob, dump[0:1, 128:256], 128)
        bcast(c_gmb, cacc[0:1, :, :].rearrange("p a b -> p (a b)"), H)
        w_const = [p.now(sDV), w_msk, p.now(sCT), p.now(sWD)]

        ABLK = {"Wq": 0, "Wk": 2, "Wv": 4, "W1": 6, "Wo": 10}

        def conv_src(blk):
            if blk < 12:
                if blk < 6:
                    nm = ["Wq", "Wq", "Wk", "Wk", "Wv", "Wv"][blk]
                    half = blk % 2
                    return dW[nm].rearrange("(f p) n -> p f n", p=128)[
                        :, :, half * 512:(half + 1) * 512]
                if blk < 10:
                    half = blk - 6
                    return dW["fus_W1"].rearrange("(f p) n -> p f n", p=128)[
                        :, :, half * 512:(half + 1) * 512]
                half = blk - 10
                return dW["Wo"].rearrange("(f p) n -> p f n", p=128)[
                    :, :, half * 512:(half + 1) * 512]
            e_, j = divmod(blk - 12, 12)
            if j < 8:
                sp, ug = divmod(j, 2)
                src = dW["w_upT"][e_].rearrange("(f p) n -> p f n", p=128)
                off = ug * 2048 + sp * 512
                return src[:, :, off:off + 512]
            dh, kfh = divmod(j - 8, 2)
            src = dW["w_downT"][e_].rearrange("(f p) n -> p f n", p=128)
            return src[:, kfh * 8:(kfh + 1) * 8, dh * 512:(dh + 1) * 512]

        scr_ready = {}

        def conv_one(blk):
            if blk < 12:
                si = blk % 2
                buf, sem, key = aring[si], sARs[si], ("aring", si)
                ssem = sCS[si]
            else:
                si = (blk - 12) % 3
                buf, sem, key = mring[si], sMRs[si], ("mring", si)
                ssem = sCS[2 + si]
            lw = ldw(buf[:, :, :], conv_src(blk),
                     waits=[ctx.get(key)], sem=sem)
            stv = st_dma(scr[blk], buf[:, :, :], waits=[lw], sem=ssem)
            ctx[key] = stv
            scr_ready[blk] = stv

        for blk in range(12):
            conv_one(blk)

        def conv_gen():
            for blk in range(12, NBLK):
                conv_one(blk)
                yield

        def aload(blk):
            si = ctx.setdefault("a_next", 0)
            ctx["a_next"] = (si + 1) % 2
            ev = ldw(aring[si][:, :, :], scr[blk],
                     waits=[ctx.get(("aring", si)), scr_ready.get(blk)],
                     sem=sARs[si])
            return si, ev

        def moe_gen(g, attn_store):
            par = g % 2
            w_xr_all = [ctx[("xr16", par, cc)] for cc in range(CPG)]

            def mload(i, blk):
                si = i % 3
                ev = ldw(mring[si][:, :, :], scr[blk],
                         waits=[ctx.get(("mring", si)), scr_ready.get(blk)],
                         sem=sMRs[si])
                ctx[("mring", si)] = None
                return si, ev

            blks = []
            for e_ in range(NE):
                base = 12 + e_ * 12
                blks += [base + j for j in range(12)]
            loaded = {}
            nload = 0

            def ensure(i):
                nonlocal nload
                while nload <= i:
                    loaded[nload] = mload(nload, blks[nload])
                    nload += 1

            def free_slot(i, ev):
                ctx[("mring", i % 3)] = ev

            gi = 0
            for e_ in range(NE):
                last_eh = None
                for sp in range(4):
                    iu, ig = gi, gi + 1
                    ensure(ig + 1 if gi + 2 < len(blks) else ig)
                    su, evu = loaded[iu]
                    sg2, evg = loaded[ig]
                    for d128 in range(4):
                        fbh = sp * 4 + d128
                        bu, bg = fbh % 2, 2 + fbh % 2
                        wmU = None
                        for kf in range(8):
                            wmU = MM(pm[bu][:, :],
                                     mring[su][:, kf, ts(d128)],
                                     xr_fm[:, par, kf, :],
                                     waits=w_xr_all + [evu, bankw(f"m{bu}")],
                                     start=(kf == 0), stop=(kf == 7))
                        wmG = None
                        for kf in range(8):
                            wmG = MM(pm[bg][:, :],
                                     mring[sg2][:, kf, ts(d128)],
                                     xr_fm[:, par, kf, :],
                                     waits=w_xr_all + [evg, bankw(f"m{bg}")],
                                     start=(kf == 0), stop=(kf == 7))
                        esg = ACT(sg_[:, fbh % 2, :], pm[bg][:, :], AF.Silu,
                                  waits=[wmG, ctx.get(("sg_free", fbh % 2))])
                        bankr(f"m{bg}", esg)
                        eh = DVE(lambda e, fbh=fbh, bu=bu: e.tensor_tensor(
                            hid[:, fbh, :], pm[bu][:, :], sg_[:, fbh % 2, :],
                            ALU.mult),
                            waits=[wmU, esg, ctx.get(("hid_free", fbh))])
                        bankr(f"m{bu}", eh)
                        ctx[("sg_free", fbh % 2)] = eh
                        last_eh = eh
                        yield
                    pe_done = p.now(sPE)
                    free_slot(iu, pe_done)
                    free_slot(ig, pe_done)
                    gi += 2
                    ensure(min(gi + 2, len(blks) - 1))
                w_h_all = [last_eh]
                for dh in range(2):
                    iA, iB = gi, gi + 1
                    ensure(min(iB + 1, len(blks) - 1))
                    sA, evA = loaded[iA]
                    sB, evB = loaded[iB]
                    for c in range(CPG):
                        bk = c % 2
                        wm = None
                        for kf in range(8):
                            wm = MM(pm[bk][:, :], hid[:, kf, ts(c)],
                                    mring[sA][:, kf, :],
                                    waits=w_h_all + [evA, bankw(f"m{bk}")],
                                    start=(kf == 0), stop=False)
                        for kf in range(8):
                            wm = MM(pm[bk][:, :], hid[:, 8 + kf, ts(c)],
                                    mring[sB][:, kf, :],
                                    waits=[evB],
                                    start=False, stop=(kf == 7))
                        wcol = w8[:, par, c, e_:e_ + 1]
                        if e_ == 0:
                            ev = DVE(lambda e, c=c, dh=dh, bk=bk, wcol=wcol:
                                     e.tensor_scalar(
                                         acc[:, c, dh * 512:(dh + 1) * 512],
                                         pm[bk][:, :], wcol, None, ALU.mult),
                                     waits=[wm, ctx[("w8", par, c)],
                                            ctx.get(("acc_free", c))])
                        else:
                            ev = DVE(lambda e, c=c, dh=dh, bk=bk, wcol=wcol:
                                     e.scalar_tensor_tensor(
                                         acc[:, c, dh * 512:(dh + 1) * 512],
                                         pm[bk][:, :], wcol,
                                         acc[:, c, dh * 512:(dh + 1) * 512],
                                         ALU.mult, ALU.add),
                                     waits=[wm, ctx[("w8", par, c)],
                                            ctx.get(("accw", c, dh))])
                        ctx[("accw", c, dh)] = ev
                        bankr(f"m{bk}", ev)
                        yield
                    pe_done = p.now(sPE)
                    free_slot(iA, pe_done)
                    free_slot(iB, pe_done)
                    gi += 2
                for fbh in range(16):
                    ctx[("hid_free", fbh)] = p.now(sPE)
            ctx[("xr_done", par)] = p.now(sPE)
            ctx[("w8_done", par)] = p.now(sDV)
            for c in range(CPG):
                tok = g * G + c * 128
                mb = 2 + c % 2
                wl = p.op("sync", lambda e, mb=mb, tok=tok: e.dma_start(
                    out=outp[:, mb, :], in_=dout[tok:tok + 128, :]),
                    waits=[attn_store[CPG - 2], attn_store[CPG - 1],
                           ctx.get(("moutp_free", mb))],
                    inc=sLM[mb - 2], dma=True)
                ev = DVE(lambda e, mb=mb, c=c: e.tensor_tensor(
                    outp[:, mb, :], outp[:, mb, :], acc[:, c, :], ALU.add),
                    waits=[wl, ctx[("accw", c, 0)], ctx[("accw", c, 1)]])
                ctx[("acc_free", c)] = ev
                st = st_dma(dout[tok:tok + 128, :], outp[:, mb, :],
                            waits=[ev], sem=sSM[mb - 2])
                ctx[("moutp_free", mb)] = st
                ctx[("w8_free", par, c)] = ev
                yield

        gen = conv_gen()
        for g in range(NGRP):
            gis = g % GPS
            t0 = g * G
            par = g % 2
            chunk_store = [None] * CPG
            attn_store = [None] * CPG

            def pump(n):
                if gen is None:
                    return
                for _ in range(n):
                    try:
                        next(gen)
                    except StopIteration:
                        break

            def drain():
                if gen is None:
                    return
                for _ in gen:
                    pass

            if not pipelined:
                drain()

            if gis == 0 and g > 0:
                gbar = [p.now(sPE), p.now(sDV), p.now(sAC)]
                ev = POOL(lambda e: e.memset(S32[:, :, :], 0.0), waits=gbar)
                for h in range(NH):
                    ctx[("S32", h)] = ev
                ev2 = POOL(lambda e: e.memset(v_fir[:, :, 0:32], 0.0),
                           waits=gbar)
                ctx["vtail"] = ev2
                ev3 = POOL(lambda e: e.memset(tails[:, :, :, :], 0.0),
                           waits=gbar)
                ctx["ctail"] = ev3

            for c in range(CPG):
                tok = t0 + c * 128
                xb = c % 2
                st_ = stat[:, xb, :]
                wx = ld(x_t[:, 0, :], dx[tok:tok + 128, :],
                        waits=w_const + [ctx.get(("x_free", xb))],
                        sem=sXp[xb])
                w_sum = DVE(lambda e, xb=xb, st_=st_: e.reduce_sum(
                    st_[:, 0:1], x_t[:, 0, :], axis=AX.X),
                    waits=[wx, ctx.get(("stat_free", xb))])
                w_sqa = ACT(dump[:, :], x_t[:, 0, 0:512], AF.Square,
                            waits=[wx, ctx.get(("stat_free_a", xb)),
                                   ctx.get(("dump_free", 0))],
                            accum=st_[:, 13:14])
                w_sqb = ACT(dump[:, :], x_t[:, 0, 512:1024], AF.Square,
                            waits=[w_sqa], accum=st_[:, 14:15])
                w_sq = DVE(lambda e, st_=st_: e.tensor_tensor(
                    st_[:, 1:2], st_[:, 13:14], st_[:, 14:15], ALU.add),
                    waits=[w_sqb])
                e1 = DVE(lambda e, st_=st_: e.tensor_scalar(
                    st_[:, 2:3], st_[:, 0:1], 1.0 / H, None, ALU.mult),
                    waits=[w_sum])
                e2 = DVE(lambda e, st_=st_: e.tensor_scalar(
                    st_[:, 3:4], st_[:, 1:2], 1.0 / H, None, ALU.mult),
                    waits=[w_sq])
                e3 = DVE(lambda e, st_=st_: e.tensor_tensor(
                    st_[:, 4:5], st_[:, 2:3], st_[:, 2:3], ALU.mult),
                    waits=[e1])
                e4 = DVE(lambda e, st_=st_: e.tensor_tensor(
                    st_[:, 5:6], st_[:, 3:4], st_[:, 4:5], ALU.subtract),
                    waits=[e2, e3])
                e5 = DVE(lambda e, st_=st_: e.tensor_scalar(
                    st_[:, 6:7], st_[:, 5:6], LN_EPS, None, ALU.add),
                    waits=[e4])
                e6 = ACT(st_[:, 7:8], st_[:, 6:7], AF.Ln, waits=[e5])
                e7 = ACT(st_[:, 8:9], st_[:, 7:8], AF.Exp, waits=[e6],
                         scale=-0.5)
                w_xn = DVE(lambda e, xb=xb, st_=st_: e.tensor_scalar(
                    xn_tok[:, 0, :], x_t[:, 0, :], st_[:, 2:3], st_[:, 8:9],
                    ALU.subtract, ALU.mult),
                    waits=[e7, ctx.get(("xn_free", xb))])
                w_op = DVE(lambda e, xb=xb: e.tensor_add(
                    outp[:, xb, :], x_t[:, 0, :], xn_tok[:, 0, :]),
                    waits=[w_xn, ctx.get(("outp_free", xb))])
                ctx[("x_free", xb)] = w_op
                stv = st_dma(dout[tok:tok + 128, :], outp[:, xb, :],
                             waits=[w_op], sem=sST[xb])
                chunk_store[c] = stv
                ctx[("outp_free", xb)] = stv

                w_sq2a = ACT(dump[:, :], xn_tok[:, 0, 0:512], AF.Square,
                             waits=[w_xn, w_sqb], accum=st_[:, 9:10])
                w_sq2b = ACT(dump[:, :], xn_tok[:, 0, 512:1024], AF.Square,
                             waits=[w_sq2a], accum=st_[:, 15:16])
                ctx[("dump_free", 0)] = w_sq2b
                w_sq2 = DVE(lambda e, st_=st_: e.tensor_tensor(
                    st_[:, 9:10], st_[:, 9:10], st_[:, 15:16], ALU.add),
                    waits=[w_sq2b])
                e8 = DVE(lambda e, st_=st_: e.tensor_scalar(
                    st_[:, 10:11], st_[:, 9:10], 1.0 / H, FFN_EPS,
                    ALU.mult, ALU.add), waits=[w_sq2])
                e9 = ACT(st_[:, 11:12], st_[:, 10:11], AF.Ln, waits=[e8])
                eA = ACT(st_[:, 12:13], st_[:, 11:12], AF.Exp, waits=[e9],
                         scale=-0.5)
                ctx[("stat_free", xb)] = eA
                wxr1 = DVE(lambda e, xb=xb, st_=st_: e.tensor_scalar(
                    xr_tok[:, 0, :], xn_tok[:, 0, :], st_[:, 12:13], None,
                    ALU.mult), waits=[eA, ctx.get(("xr_free", xb))])
                ctx[("stat_free_a", xb)] = wxr1
                wxr = DVE(lambda e, xb=xb: e.tensor_mul(
                    xr_tok[:, 0, :], xr_tok[:, 0, :], c_gmb[:, :]),
                    waits=[wxr1])

                a2f = [ctx.get("arena2_free")] + ctx.get("xr32_rd", [])
                for bk in range(2):
                    tr = None
                    for q4 in range(4):
                        f = bk * 4 + q4
                        tr = PET(pa[bk][:, ts(q4)], xr_tok[:, 0, ts(f)],
                                 waits=[wxr, bankw(f"a{bk}")], f32=True)
                    ev = ACT(xr_fm32.rearrange("p a b -> p (a b)")
                             [:, bk * 512:(bk + 1) * 512],
                             pa[bk][:, :], waits=[tr] + a2f)
                    bankr(f"a{bk}", ev)
                w_xr32 = ctx[("bk", "a1")]
                ctx[("xr_free", xb)] = w_xr32
                w_xr16 = DVE(lambda e, c=c, par=par: e.tensor_copy(
                    xr_fm[:, par, :, c * 128:(c + 1) * 128],
                    xr_fm32[:, :, :]),
                    waits=[w_xr32, ctx.get(("xr_done", par))])
                ctx[("xr16", par, c)] = w_xr16

                wrm = None
                for kf in range(8):
                    wrm = MM(pa[2][0:NE, 0:128], r_wr[:, kf, :],
                             xr_fm32[:, kf, :],
                             waits=[w_xr32, bankw("a2")] + w_pre,
                             start=(kf == 0), stop=(kf == 7))
                ctx["xr32_rd"] = [wrm, w_xr16]
                wse = ACT(sc_fm[:, :], pa[2][0:NE, 0:128], waits=[wrm])
                bankr("a2", wse)
                wst = PET(pa[2][:, 128:128 + NE], sc_fm[:, :],
                          waits=[wse, bankw("a2")], f32=True)
                wsc = ACT(ext[:, 0:NE], pa[2][:, 128:128 + NE],
                          waits=[wst, ctx.get("ext_free_a")])
                bankr("a2", wsc)
                wm1 = DVE(lambda e: e.reduce_max(
                    ext[:, 8:9], ext[:, 0:NE], axis=AX.X), waits=[wsc])
                weq1 = DVE(lambda e: e.tensor_scalar(
                    ext[:, 16:16 + NE], ext[:, 0:NE], ext[:, 8:9], None,
                    ALU.is_equal), waits=[wm1])
                wmsk = DVE(lambda e: e.scalar_tensor_tensor(
                    ext[:, 24:24 + NE], ext[:, 16:16 + NE], -1e30,
                    ext[:, 0:NE], ALU.mult, ALU.add), waits=[weq1])
                wm2 = DVE(lambda e: e.reduce_max(
                    ext[:, 9:10], ext[:, 24:24 + NE], axis=AX.X), waits=[wmsk])
                wd_ = DVE(lambda e: e.tensor_tensor(
                    ext[:, 10:11], ext[:, 8:9], ext[:, 9:10], ALU.subtract),
                    waits=[wm2])
                ww1 = ACT(ext[:, 11:12], ext[:, 10:11], AF.Sigmoid, waits=[wd_])
                ww2 = DVE(lambda e: e.tensor_scalar(
                    ext[:, 12:13], ext[:, 11:12], -1.0, 1.0, ALU.mult,
                    ALU.add), waits=[ww1])
                weq2 = DVE(lambda e: e.tensor_scalar(
                    ext[:, 32:32 + NE], ext[:, 0:NE], ext[:, 9:10], None,
                    ALU.is_equal), waits=[wm2])
                wnq1 = DVE(lambda e: e.tensor_scalar(
                    ext[:, 40:40 + NE], ext[:, 16:16 + NE], -1.0, 1.0,
                    ALU.mult, ALU.add), waits=[weq1])
                weq2b = DVE(lambda e: e.tensor_tensor(
                    ext[:, 32:32 + NE], ext[:, 32:32 + NE], ext[:, 40:40 + NE],
                    ALU.mult), waits=[weq2, wnq1])
                wpa = DVE(lambda e: e.tensor_scalar(
                    ext[:, 48:48 + NE], ext[:, 16:16 + NE], ext[:, 11:12],
                    None, ALU.mult), waits=[ww2])
                ww8 = DVE(lambda e, c=c, par=par: e.scalar_tensor_tensor(
                    w8[:, par, c, :], ext[:, 32:32 + NE], ext[:, 12:13],
                    ext[:, 48:48 + NE], ALU.mult, ALU.add),
                    waits=[weq2b, wpa, ctx.get(("w8_free", par, c)),
                           ctx.get(("w8_done", par))])
                ctx[("w8", par, c)] = ww8
                ctx["ext_free_a"] = ww8

                for bk in range(2):
                    tr = None
                    for q4 in range(4):
                        f = bk * 4 + q4
                        tr = PET(pa[bk][:, ts(q4)], xn_tok[:, 0, ts(f)],
                                 waits=[w_xn, bankw(f"a{bk}")], f32=True)
                    ev = ACT(xn_fm[:, bk * 4:(bk + 1) * 4, c * 128:(c + 1) * 128],
                             pa[bk][:, :], waits=[tr, ctx.get("xnfm_free")])
                    bankr(f"a{bk}", ev)
                ctx[("xn_free", xb)] = ctx[("bk", "a1")]
                ctx[("xnT", c)] = ctx[("bk", "a1")]
                pump(13)

            w_xn_all = [ctx[("xnT", cc)] for cc in range(CPG)]

            ev_silu = {}
            unit_blk = {}

            def load_unit(nm):
                b0 = ABLK[nm]
                nb = 4 if nm == "W1" else 2
                unit_blk[nm] = [aload(b0 + i) for i in range(nb)]

            load_unit("Wq")

            def proj_stream(nm, dst_fm, fir_dst=None):
                ui = {"Wq": 0, "Wk": 1, "Wv": 2}[nm]
                evac_evs = []
                for half in range(2):
                    si, wrdy = unit_blk[nm][half]
                    for fb4 in range(4):
                        fb = half * 4 + fb4
                        pbk = fb % 2
                        wm = None
                        for kf in range(8):
                            wm = MM(pa[pbk][:, :],
                                    aring[si][:, kf, ts(fb4)],
                                    xn_fm[:, kf, :],
                                    waits=w_xn_all + [wrdy, bankw(f"a{pbk}")],
                                    start=(kf == 0), stop=(kf == 7))
                        ev = ACT(raw[:, fb, 4:4 + G], pa[pbk][:, :],
                                 waits=[wm, ctx.get("raw_free"),
                                        ctx.get("ctail")])
                        bankr(f"a{pbk}", ev)
                        evac_evs.append(ev)
                    pump(8)
                ctx[("aring", unit_blk[nm][0][0])] = p.now(sPE)
                ctx[("aring", unit_blk[nm][1][0])] = p.now(sPE)
                wtl = DVE(lambda e, ui=ui: e.tensor_copy(
                    raw[:, :, 1:4], tails[:, ui, :, 1:4]),
                    waits=evac_evs + [ctx.get("ctail"), ctx.get("raw_free")])
                wsv = DVE(lambda e, ui=ui: e.tensor_copy(
                    tails[:, ui, :, 1:4], raw[:, :, 1 + G:4 + G]),
                    waits=evac_evs + [wtl])
                ctx["wsv_" + ("v" if fir_dst is not None else "x")] = wsv
                chain = [None] * 8
                cw = c_cw[nm]
                silu_list = []
                for pr in range(4):
                    for i_tap in range(4):
                        for fb in (2 * pr, 2 * pr + 1):
                            ca = cacc[:, fb % 2, :]
                            win = raw[:, fb, 4 - i_tap: 4 - i_tap + G]
                            ct = 3 - i_tap
                            col = cw[:, fb * 4 + ct: fb * 4 + ct + 1]
                            if i_tap == 0:
                                chain[fb] = DVE(
                                    lambda e, ca=ca, win=win, col=col:
                                    e.tensor_scalar(ca, win, col, None,
                                                    ALU.mult),
                                    waits=[wtl, wsv,
                                           ctx.get(("cacc_free", fb % 2)),
                                           evac_evs[fb]])
                            else:
                                chain[fb] = DVE(
                                    lambda e, ca=ca, win=win, col=col:
                                    e.scalar_tensor_tensor(ca, win, col, ca,
                                                           ALU.mult, ALU.add),
                                    waits=[chain[fb]])
                    for fb in (2 * pr, 2 * pr + 1):
                        ca = cacc[:, fb % 2, :]
                        if fir_dst is None:
                            ev = ACT(dst_fm[:, fb, :], ca, AF.Silu,
                                     waits=[chain[fb],
                                            ctx.get(("qk_free", nm))])
                        else:
                            ev = ACT(fir_dst[:, fb, 32:32 + G], ca, AF.Silu,
                                     waits=[chain[fb], ctx.get("vtail")])
                        ctx[("cacc_free", fb % 2)] = ev
                        ev_silu[(nm, fb)] = ev
                        silu_list.append(ev)
                return silu_list, p.now(sDV)

            sl_q, cd = proj_stream("Wq", q_fm)
            ctx["raw_free"] = cd
            load_unit("Wk")
            pump(2)
            sl_k, cd = proj_stream("Wk", k_fm)
            ctx["raw_free"] = cd
            load_unit("Wv")
            pump(2)
            sl_v, cd = proj_stream("Wv", None, fir_dst=v_fir)
            ctx["raw_free"] = cd
            ctx["last_wsv"] = ctx.get("wsv_v")
            pump(2)

            for which in ("b", "f"):
                rw = r_wb if which == "b" else r_wf
                row = 0 if which == "b" else 1
                wm = None
                for kf in range(8):
                    wm = MM(pa[2][0:NH, 0:G], rw[:, kf, :],
                            xn_fm[:, kf, :],
                            waits=w_xn_all + [bankw("a2")] + w_pre,
                            start=(kf == 0), stop=(kf == 7))
                if which == "b":
                    ev = ACT(bf_fm[:, row, :], pa[2][0:NH, 0:G], AF.Sigmoid,
                             waits=[wm, ctx.get("bffm_free")])
                else:
                    ev = ACT(bf_fm[:, row, :], pa[2][0:NH, 0:G], AF.Sigmoid,
                             waits=[wm, ctx.get("bffm_free")],
                             bias=c_fb[:, 0:1])
                bankr("a2", ev)
                ctx[("bf", row)] = ev
            pump(8)

            wf2mm = []
            for half in range(2):
                w1blk = [aload(6 + half * 2), aload(7 + half * 2)]
                h_evs = []
                for q2 in range(2):
                    si, wrdy = w1blk[q2]
                    for fb4 in range(4):
                        fbg = half * 8 + q2 * 4 + fb4
                        fbr = q2 * 4 + fb4
                        pbk = fbr % 2
                        wm = None
                        for kf in range(8):
                            wm = MM(pa[pbk][:, :],
                                    aring[si][:, kf, ts(fb4)],
                                    xn_fm[:, kf, :],
                                    waits=w_xn_all + [wrdy, bankw(f"a{pbk}")],
                                    start=(kf == 0), stop=(kf == 7))
                        ev = ACT(raw[:, fbr, 4:4 + G], pa[pbk][:, :], AF.Gelu,
                                 waits=[wm, ctx.get("raw_fus_free"),
                                        ctx.get("last_wsv")] +
                                 (sl_v if half == 0 else []),
                                 bias=c_b1[:, fbg:fbg + 1])
                        bankr(f"a{pbk}", ev)
                        h_evs.append(ev)
                    pump(8)
                ctx[("aring", w1blk[0][0])] = p.now(sPE)
                ctx[("aring", w1blk[1][0])] = p.now(sPE)
                wm = None
                for kf in range(8):
                    wm = MM(pa[2][0:32, 0:G], r_w2[:, half * 8 + kf, :],
                            raw[:, kf, 4:4 + G],
                            waits=h_evs + [bankw("a2")] + w_pre,
                            start=(half == 0 and kf == 0),
                            stop=(kf == 7))
                wf2mm.append(wm)
                ctx["raw_fus_free"] = wm
            load_unit("Wo")
            wf2 = DVE(lambda e: e.tensor_scalar(
                fus_fm[:, :], pa[2][0:32, 0:G], c_b2[:, 0:1], None, ALU.add),
                waits=[wf2mm[1], ctx[("xr16", par, CPG - 1)]])
            bankr("a2", wf2)
            w_fus = wf2
            ctx["raw_fus_done"] = p.now(sPE)
            pump(8)

            fir_ready = {}
            fchain = {}
            base = 32
            for pr in range(4):
                for i_tap in range(33):
                    for fb in (2 * pr, 2 * pr + 1):
                        ca = cacc[:, fb % 2, 0:G]
                        if i_tap == 0:
                            cs = c_fs[:, fb * 3 + 2: fb * 3 + 3]
                            win = v_fir[:, fb, base:base + G]
                            fchain[fb] = DVE(
                                lambda e, ca=ca, win=win, cs=cs:
                                e.tensor_scalar(ca, win, cs, None, ALU.mult),
                                waits=[ev_silu[("Wv", fb)],
                                       ctx.get(("cacc_free", fb % 2)),
                                       ctx.get("fir_free"),
                                       ctx.get("raw_fus_done")])
                        elif i_tap == 1:
                            cs = c_fs[:, fb * 3 + 1: fb * 3 + 2]
                            win = v_fir[:, fb, base - 1:base - 1 + G]
                            fchain[fb] = DVE(
                                lambda e, ca=ca, win=win, cs=cs:
                                e.scalar_tensor_tensor(ca, win, cs, ca,
                                                       ALU.mult, ALU.add),
                                waits=[fchain[fb]])
                        elif i_tap == 2:
                            cs = c_fs[:, fb * 3: fb * 3 + 1]
                            win = v_fir[:, fb, base - 2:base - 2 + G]
                            fchain[fb] = DVE(
                                lambda e, ca=ca, win=win, cs=cs, fb=fb:
                                e.scalar_tensor_tensor(fir_s[:, fb, :], win,
                                                       cs, ca, ALU.mult,
                                                       ALU.add),
                                waits=[fchain[fb]])
                            cl = c_fl[:, fb * 31 + 30: fb * 31 + 31]
                            winl = v_fir[:, fb, base:base + G]
                            fchain[fb] = DVE(
                                lambda e, ca=ca, winl=winl, cl=cl:
                                e.tensor_scalar(ca, winl, cl, None, ALU.mult),
                                waits=[fchain[fb]])
                        else:
                            lt = i_tap - 2
                            cl = c_fl[:, fb * 31 + 30 - lt: fb * 31 + 31 - lt]
                            win = v_fir[:, fb, base - lt: base - lt + G]
                            if lt < 30:
                                fchain[fb] = DVE(
                                    lambda e, ca=ca, win=win, cl=cl:
                                    e.scalar_tensor_tensor(ca, win, cl, ca,
                                                           ALU.mult, ALU.add),
                                    waits=[fchain[fb]])
                            else:
                                ev = DVE(
                                    lambda e, ca=ca, win=win, cl=cl, fb=fb:
                                    e.scalar_tensor_tensor(
                                        raw[:, fb, 4:4 + G], win, cl, ca,
                                        ALU.mult, ALU.add),
                                    waits=[fchain[fb]])
                                fchain[fb] = ev
                                ctx[("cacc_free", fb % 2)] = ev
                                fir_ready[fb] = ev
            pump(6)

            w_a1dv = p.now(sDV)
            for c in range(CPG):
                tb1 = PET(pt_a[:, 0:NH], bf_fm[:, 0, c * 128:(c + 1) * 128],
                          waits=[ctx[("bf", 0)], bankw("t0")])
                tb2 = PET(pt_a[:, 128:128 + NH],
                          bf_fm[:, 1, c * 128:(c + 1) * 128],
                          waits=[ctx[("bf", 1)]])
                wbe = ACT(beta_b[:, c, :], pt_a[:, 0:NH],
                          waits=[tb2, ctx.get(("beta_free", c))])
                wfe = ACT(ext[:, 0:NH], pt_a[:, 128:128 + NH],
                          waits=[tb2, w_a1dv, ctx.get("ext_free")])
                bankr("t0", wfe)
                wb3 = PET(pa[2][:, 256:256 + 32],
                          fus_fm[:, c * 128:(c + 1) * 128],
                          waits=[w_fus, bankw("a2")], f32=True)
                wfu = ACT(fus_tok[:, :], pa[2][:, 256:256 + 32],
                          waits=[wb3, ctx.get("fus_tok_free")])
                bankr("a2", wfu)
                ctx[("betaT", c)] = wbe
                s0 = fus_tok[:, 0:32:4]
                s1 = fus_tok[:, 1:32:4]
                s2 = fus_tok[:, 2:32:4]
                s3 = fus_tok[:, 3:32:4]
                wmx = DVE(lambda e, s0=s0, s1=s1: e.tensor_tensor(
                    ext[:, 8:16], s0, s1, ALU.max), waits=[wfu, w_a1dv])
                wmx = DVE(lambda e, s2=s2: e.tensor_tensor(
                    ext[:, 8:16], ext[:, 8:16], s2, ALU.max), waits=[wmx])
                wd0 = DVE(lambda e, s0=s0: e.tensor_tensor(
                    ext[:, 16:24], s0, ext[:, 8:16], ALU.subtract), waits=[wmx])
                wd1 = DVE(lambda e, s1=s1: e.tensor_tensor(
                    ext[:, 24:32], s1, ext[:, 8:16], ALU.subtract), waits=[wmx])
                wd2 = DVE(lambda e, s2=s2: e.tensor_tensor(
                    ext[:, 32:40], s2, ext[:, 8:16], ALU.subtract), waits=[wmx])
                wex = ACT(ext[:, 16:40], ext[:, 16:40], AF.Exp,
                          waits=[wd0, wd1, wd2])
                wsm = DVE(lambda e: e.tensor_tensor(
                    ext[:, 40:48], ext[:, 16:24], ext[:, 24:32], ALU.add),
                    waits=[wex])
                wsm = DVE(lambda e: e.tensor_tensor(
                    ext[:, 40:48], ext[:, 40:48], ext[:, 32:40], ALU.add),
                    waits=[wsm])
                wrc = DVE(lambda e: e.reciprocal(ext[:, 48:56], ext[:, 40:48]),
                          waits=[wsm])
                ws3 = ACT(ext[:, 56:64], s3, AF.Sigmoid, waits=[wfu])
                wfl = DVE(lambda e: e.tensor_scalar(
                    ext[:, 0:NH], ext[:, 0:NH], -0.2, 1.0, ALU.mult, ALU.add),
                    waits=[wfe])
                wpv = DVE(lambda e, c=c: e.tensor_tensor(
                    coef[:, c, 3, :], ext[:, 56:64], ext[:, 0:NH], ALU.mult),
                    waits=[ws3, wfl, ctx.get(("coef_free", c))])
                wq1 = DVE(lambda e, c=c: e.tensor_scalar(
                    ext[:, 56:64], coef[:, c, 3, :], -1.0, 1.0, ALU.mult,
                    ALU.add), waits=[wpv])
                wcw = DVE(lambda e: e.tensor_tensor(
                    ext[:, 48:56], ext[:, 48:56], ext[:, 56:64], ALU.mult),
                    waits=[wq1, wrc])
                for j, sj in enumerate((slice(16, 24), slice(24, 32),
                                        slice(32, 40))):
                    DVE(lambda e, c=c, j=j, sj=sj: e.tensor_tensor(
                        coef[:, c, j, :], ext[:, sj], ext[:, 48:56], ALU.mult),
                        waits=[wcw])
                ctx[("coef", c)] = p.now(sDV)
                ctx["ext_free"] = p.now(sDV)
                ctx["fus_tok_free"] = wd2
                pump(8)
            ctx["bffm_free"] = p.now(sPE)
            ctx["arena2_free"] = p.now(sPE)
            ctx["ext_free_a"] = p.now(sDV)

            L_ = dict(
                p=p, ctx=ctx, MM=MM, PET=PET, ACT=ACT, DVE=DVE,
                bankw=bankw, bankr=bankr,
                q_fm=q_fm, k_fm=k_fm, v_fir=v_fir, fir_s=fir_s, raw=raw,
                fir_ready=fir_ready,
                pa=pa, pt_a=pt_a,
                d_tok=d_tok, d_fm3=d_fm3, d_knt=d_knt, d_kbt=d_kbt,
                d_vbt=d_vbt, d_AAT=d_AAT, d_PP=d_PP, d_T=d_T, d_uw=d_uw,
                d_alT=d_alT, d_unew=d_unew, d_o2=d_o2, d_ft=d_ft,
                o_tmp=o_tmp, o2_fm=o2_fm, dstat=dstat, dump=dump,
                S32=S32, s16t=s16t, beta_b=beta_b, coef=coef,
                c_mslu=c_mslu, c_miu=c_miu, c_id16=c_id16, c_pfb=c_pfb,
                c_nob=c_nob, ev_silu=ev_silu, sPE=sPE, sDV=sDV, sAC=sAC,
                sLA=sLA, sSE=sSE, ld=ld, st_dma=st_dma, dout=dout,
                chunk_store=chunk_store, attn_store=attn_store,
                t0=t0, aring=aring, unit_blk=unit_blk, outp=outp,
            )
            for c in range(CPG):
                for h in range(NH):
                    emit_delta(p, ctx, L_, c, h)
                emit_epilogue(p, ctx, L_, c, g)
                pump(6)
            pump(6)
            ctx[("aring", unit_blk["Wo"][0][0])] = p.now(sPE)
            ctx[("aring", unit_blk["Wo"][1][0])] = p.now(sPE)

            ev = DVE(lambda e: e.tensor_copy(
                v_fir[:, :, 0:32], v_fir[:, :, G:G + 32]),
                waits=sl_v + [p.now(sPE)])
            ctx["vtail"] = ev
            ctx["ctail"] = p.now(sDV)
            ctx[("qk_free", "Wq")] = p.now(sPE)
            ctx[("qk_free", "Wk")] = p.now(sPE)
            ctx["xnfm_free"] = p.now(sPE)
            ctx["fir_free"] = p.now(sPE)
            ctx["raw_free"] = p.now(sPE)
            ctx[("cacc_free", 0)] = p.now(sDV)
            ctx[("cacc_free", 1)] = p.now(sDV)

            drain()
            gen = moe_gen(g, list(attn_store))
            if g == NGRP - 1:
                drain()
                gen = None

        allsems = ([sCT, sWD, sPE, sDV, sAC, sPO] + sXp + sLA + sLM +
                   sST + sSE + sSM + sARs + sMRs + sCS)
        p.op("sync", lambda e: e.nop(), waits=[p.now(s) for s in allsems])

        sems = [stack.enter_context(nc.semaphore(n)) for n in p.names]
        with nc.Block() as block:
            block.tensor(lambda e: p.replay("pe", e, sems))
            block.vector(lambda e: p.replay("dve", e, sems))
            block.scalar(lambda e: p.replay("act", e, sems))
            block.gpsimd(lambda e: p.replay("pool", e, sems))
            block.sync(lambda e: p.replay("sync", e, sems))
    return nc


def emit_delta(p, ctx, L_, c, h):
    MM, PET, ACT, DVE = L_["MM"], L_["PET"], L_["ACT"], L_["DVE"]
    bankw, bankr = L_["bankw"], L_["bankr"]
    q_fm, k_fm, v_fir = L_["q_fm"], L_["k_fm"], L_["v_fir"]
    fir_s, raw, fir_ready = L_["fir_s"], L_["raw"], L_["fir_ready"]
    pa, pt_a = L_["pa"], L_["pt_a"]
    d_tok, d_fm3 = L_["d_tok"], L_["d_fm3"]
    d_knt, d_kbt, d_vbt = L_["d_knt"], L_["d_kbt"], L_["d_vbt"]
    d_AAT, d_PP, d_T = L_["d_AAT"], L_["d_PP"], L_["d_T"]
    d_uw, d_alT, d_unew, d_o2 = L_["d_uw"], L_["d_alT"], L_["d_unew"], L_["d_o2"]
    d_ft, o_tmp, o2_fm = L_["d_ft"], L_["o_tmp"], L_["o2_fm"]
    dstat, dump = L_["dstat"], L_["dump"]
    S32, s16t = L_["S32"], L_["s16t"]
    beta_b, coef = L_["beta_b"], L_["coef"]
    c_mslu, c_miu, c_id16, c_pfb, c_nob = (L_["c_mslu"], L_["c_miu"],
                                           L_["c_id16"], L_["c_pfb"],
                                           L_["c_nob"])
    ev_silu = L_["ev_silu"]
    sPE, sDV = L_["sPE"], L_["sDV"]
    tw = c * 128
    st = dstat[:, :]
    free = ctx.get("dfree")

    t1 = None
    for j, src in enumerate((q_fm[:, h, tw:tw + 128], k_fm[:, h, tw:tw + 128],
                             v_fir[:, h, 32 + tw:32 + tw + 128])):
        t1 = PET(pt_a[:, ts(j)], src,
                 waits=[ev_silu[("Wq", h)], ev_silu[("Wk", h)],
                        ev_silu[("Wv", h)], bankw("t0"), free])
    e1 = ACT(d_tok[:, :, :].rearrange("p a b -> p (a b)"),
             pt_a[:, 0:384], waits=[t1, free])
    bankr("t0", e1)
    sq = ACT(dump[:, 0:128], d_tok[:, 0, :], AF.Square, waits=[e1],
             accum=st[:, 0:1])
    sk = ACT(dump[:, 128:256], d_tok[:, 1, :], AF.Square, waits=[e1],
             accum=st[:, 1:2])
    ee = DVE(lambda e: e.tensor_scalar(st[:, 2:4], st[:, 0:2], L2_EPS, None,
                                       ALU.add), waits=[sq, sk])
    el = ACT(st[:, 4:6], st[:, 2:4], AF.Ln, waits=[ee])
    er = ACT(st[:, 6:8], st[:, 4:6], AF.Exp, waits=[el], scale=-0.5)
    bcol = beta_b[:, c, h:h + 1]
    ebeta = ctx[("betaT", c)]
    eq = DVE(lambda e: e.tensor_scalar(d_tok[:, 0, :], d_tok[:, 0, :],
                                       st[:, 6:7], None, ALU.mult),
             waits=[er])
    ek = DVE(lambda e: e.tensor_scalar(d_knt[:, :], d_tok[:, 1, :],
                                       st[:, 7:8], None, ALU.mult),
             waits=[er, free])
    ekb = DVE(lambda e: e.tensor_scalar(d_kbt[:, :], d_knt[:, :],
                                        bcol, None, ALU.mult),
              waits=[ek, ebeta, free])
    evb = DVE(lambda e: e.tensor_scalar(d_vbt[:, :], d_tok[:, 2, :],
                                        bcol, None, ALU.mult),
              waits=[e1, ebeta, free])
    t2 = None
    for j, src in enumerate((d_tok[:, 0, :], d_knt[:, :], d_kbt[:, :])):
        t2 = PET(pt_a[:, ts(j)], src, waits=[eq, ek, ekb, bankw("t0")])
    e2 = ACT(d_fm3[:, :, :].rearrange("p a b -> p (a b)"),
             pt_a[:, 0:384], waits=[t2, free])
    bankr("t0", e2)
    qn_f, kn_f, kb_f = (d_fm3[:, 0, :], d_fm3[:, 1, :], d_fm3[:, 2, :])
    a1 = MM(pa[0][:, 0:128], kb_f, kn_f, waits=[e2, bankw("a0")])
    a2 = MM(pa[0][:, 128:256], kn_f, kb_f, waits=[e2])
    eA = DVE(lambda e: e.tensor_mul(
        d_AAT[:, :, :].rearrange("p a b -> p (a b)"),
        pa[0][:, 0:256], c_mslu[:, :, :].rearrange("p a b -> p (a b)")),
        waits=[a2, free])
    bankr("a0", eA)
    prevP, prevT = d_AAT[:, 0, :], d_AAT[:, 1, :]
    prev_ev = eA
    for lvl in range(5):
        m1 = MM(pa[0][:, 0:128], prevT, prevP, waits=[prev_ev, bankw("a0")])
        m2 = MM(pa[0][:, 128:256], prevP, prevT, waits=[prev_ev])
        pev = ACT(d_PP[:, lvl, :, :].rearrange("p a b -> p (a b)"),
                  pa[0][:, 0:256], waits=[m2, free])
        bankr("a0", pev)
        prevP = d_PP[:, lvl, 0, :]
        prevT = d_PP[:, lvl, 1, :]
        prev_ev = pev
    m64 = MM(pa[0][:, 256:384], d_PP[:, 4, 0, :], d_PP[:, 4, 1, :],
             waits=[prev_ev, bankw("a0")])
    eT = DVE(lambda e: e.tensor_add(d_T[:, 0, :], pa[0][:, 256:384],
                                    c_id16[:, :]), waits=[m64, free])
    bankr("a0", eT)
    cur = 0
    for kk in [4, 3, 2, 1, 0, -1]:
        lhsT = d_AAT[:, 0, :] if kk < 0 else d_PP[:, kk, 0, :]
        mp = MM(pa[0][:, 384:512], lhsT, d_T[:, cur, :],
                waits=[eT, bankw("a0")])
        eT = DVE(lambda e, cur=cur: e.tensor_add(
            d_T[:, 1 - cur, :], pa[0][:, 384:512], d_T[:, cur, :]),
            waits=[mp])
        bankr("a0", eT)
        cur = 1 - cur
    Tf = d_T[:, cur, :]
    cs16 = ACT(s16t[:, :], S32[:, h, :],
               waits=[ctx[("S32", h)], ctx.get("s16_free")])
    mu = MM(pa[1][:, 0:128], Tf, d_vbt[:, :],
            waits=[eT, evb, bankw("a1")])
    mw = MM(pa[1][:, 128:256], d_kbt[:, :], Tf, waits=[eT, ekb])
    euw = ACT(d_uw[:, :, :].rearrange("p a b -> p (a b)"),
              pa[1][:, 0:256], waits=[mw, free])
    bankr("a1", euw)
    mal = MM(pa[1][:, 256:384], kn_f, qn_f, waits=[e2, bankw("a1")])
    eal = DVE(lambda e: e.tensor_mul(d_alT[:, :], pa[1][:, 256:384],
                                     c_miu[:, :]), waits=[mal, free])
    bankr("a1", eal)
    mws = MM(pa[1][:, 384:512], d_uw[:, 1, :], s16t[:, :],
             waits=[euw, cs16, bankw("a1")])
    eun = DVE(lambda e: e.tensor_tensor(d_unew[:, :], d_uw[:, 0, :],
                                        pa[1][:, 384:512], ALU.subtract),
              waits=[mws, free])
    bankr("a1", eun)
    mo1 = MM(pa[2][:, 0:128], qn_f, s16t[:, :],
             waits=[e2, cs16, bankw("a2")], start=True, stop=False)
    mo = MM(pa[2][:, 0:128], d_alT[:, :], d_unew[:, :],
            waits=[eal, eun], start=False, stop=True)
    ctx["s16_free"] = mo
    mds = MM(pa[1][:, 0:128], d_knt[:, :], d_unew[:, :],
             waits=[ek, eun, bankw("a1")])
    es1 = DVE(lambda e: e.tensor_tensor(S32[:, h, :], S32[:, h, :],
                                        pa[1][:, 0:128], ALU.add),
              waits=[mds, ctx[("S32", h)]])
    ctx[("S32", h)] = es1
    bankr("a1", es1)
    tf = PET(pt_a[:, 0:128], fir_s[:, h, tw:tw + 128],
             waits=[fir_ready[h], bankw("t0")])
    tf = PET(pt_a[:, 128:256], raw[:, h, 4 + tw:4 + tw + 128],
             waits=[fir_ready[h]])
    eft = ACT(d_ft[:, :, :].rearrange("p a b -> p (a b)"),
              pt_a[:, 0:256], waits=[tf, free])
    bankr("t0", eft)
    wcf = ctx[("coef", c)]
    cc0 = coef[:, c, 0, h:h + 1]
    cc1 = coef[:, c, 1, h:h + 1]
    cc2 = coef[:, c, 2, h:h + 1]
    ccp = coef[:, c, 3, h:h + 1]
    eo = DVE(lambda e: e.tensor_scalar(o_tmp[:, :], d_ft[:, 0, :],
                                       cc0, None, ALU.mult),
             waits=[eft, wcf, free])
    eo = DVE(lambda e: e.scalar_tensor_tensor(
        o_tmp[:, :], d_ft[:, 1, :], cc1, o_tmp[:, :],
        ALU.mult, ALU.add), waits=[eo])
    eo = DVE(lambda e: e.scalar_tensor_tensor(
        o_tmp[:, :], pa[2][:, 0:128], cc2, o_tmp[:, :],
        ALU.mult, ALU.add), waits=[eo, mo])
    eo = DVE(lambda e: e.scalar_tensor_tensor(
        o_tmp[:, :], d_tok[:, 2, :], ccp, o_tmp[:, :],
        ALU.mult, ALU.add), waits=[eo])
    bankr("a2", eo)
    sq1 = ACT(dump[:, 256:384], o_tmp[:, :], AF.Square, waits=[eo],
              accum=st[:, 8:9])
    em = DVE(lambda e: e.tensor_scalar(st[:, 9:10], st[:, 8:9], 1.0 / 128,
                                       RMS_EPS, ALU.mult, ALU.add),
             waits=[sq1])
    el1 = ACT(st[:, 10:11], st[:, 9:10], AF.Ln, waits=[em])
    er1 = ACT(st[:, 11:12], st[:, 10:11], AF.Exp, waits=[el1], scale=-0.5)
    eo1 = DVE(lambda e: e.scalar_tensor_tensor(
        o_tmp[:, :], o_tmp[:, :], st[:, 11:12], c_pfb[:, :],
        ALU.mult, ALU.mult), waits=[er1, free])
    sq2 = ACT(dump[:, 384:512], o_tmp[:, :], AF.Square, waits=[eo1],
              accum=st[:, 12:13])
    em2 = DVE(lambda e: e.tensor_scalar(st[:, 13:14], st[:, 12:13], 1.0 / 128,
                                        RMS_EPS, ALU.mult, ALU.add),
              waits=[sq2])
    el2 = ACT(st[:, 14:15], st[:, 13:14], AF.Ln, waits=[em2])
    er2 = ACT(st[:, 15:16], st[:, 14:15], AF.Exp, waits=[el2], scale=-0.5)
    eo2 = DVE(lambda e: e.scalar_tensor_tensor(
        d_o2[:, :], o_tmp[:, :], st[:, 15:16], c_nob[:, :],
        ALU.mult, ALU.mult), waits=[er2, free])
    to2 = PET(pt_a[:, 0:128], d_o2[:, :], waits=[eo2, bankw("t0")])
    ef = ACT(o2_fm[:, h, :], pt_a[:, 0:128],
             waits=[to2, ctx.get(("o2_free", h))])
    bankr("t0", ef)
    ctx[("o2", h)] = ef
    ctx["dfree"] = ef


def emit_epilogue(p, ctx, L_, c, g):
    sPE, sDV = L_["sPE"], L_["sDV"]
    MM, DVE = L_["MM"], L_["DVE"]
    bankw, bankr = L_["bankw"], L_["bankr"]
    pa, o2_fm, outp = L_["pa"], L_["o2_fm"], L_["outp"]
    aring, unit_blk, dout = L_["aring"], L_["unit_blk"], L_["dout"]
    st_dma = L_["st_dma"]
    chunk_store, attn_store = L_["chunk_store"], L_["attn_store"]
    t0 = L_["t0"]
    tok = t0 + c * 128
    xb = c % 2
    o2evs = [ctx[("o2", h)] for h in range(NH)]
    wl = p.op("sync", lambda e, xb=xb, tok=tok: e.dma_start(
        out=outp[:, xb, :], in_=dout[tok:tok + 128, :]),
        waits=[chunk_store[CPG - 2], chunk_store[CPG - 1],
               ctx.get(("outp_free", xb))],
        inc=L_["sLA"][xb], dma=True)
    adds = []
    for nh in range(2):
        si, wrdy = unit_blk["Wo"][nh]
        bk = nh
        wm = None
        for kf in range(8):
            wm = MM(pa[bk][:, :], o2_fm[:, kf, :],
                    aring[si][:, kf, :],
                    waits=o2evs + [wrdy, bankw(f"a{bk}")],
                    start=(kf == 0), stop=(kf == 7))
        ev = DVE(lambda e, xb=xb, nh=nh, bk=bk: e.tensor_tensor(
            outp[:, xb, nh * 512:(nh + 1) * 512],
            outp[:, xb, nh * 512:(nh + 1) * 512],
            pa[bk][:, :], ALU.add), waits=[wm, wl])
        bankr(f"a{bk}", ev)
        adds.append(ev)
    st = st_dma(dout[tok:tok + 128, :], outp[:, xb, :], waits=adds,
                sem=L_["sSE"][xb])
    attn_store[c] = st
    ctx[("outp_free", xb)] = st
    for h in range(NH):
        ctx[("o2_free", h)] = p.now(sPE)
    ctx[("beta_free", c)] = p.now(sDV)
    ctx[("coef_free", c)] = p.now(sDV)


def make_inputs(i):
    base = {
        "Wq": np.ascontiguousarray(i["L0_attn_Wq"], np.float32),
        "Wk": np.ascontiguousarray(i["L0_attn_Wk"], np.float32),
        "Wv": np.ascontiguousarray(i["L0_attn_Wv"], np.float32),
        "Wo": np.ascontiguousarray(i["L0_attn_Wo"], np.float32),
        "fus_W1": np.ascontiguousarray(i["L0_attn_fus_W1"], np.float32),
        "fus_W2": np.ascontiguousarray(i["L0_attn_fus_W2"], np.float32),
        "Wb": np.ascontiguousarray(i["L0_attn_Wb"], np.float32),
        "floor_W": np.ascontiguousarray(i["L0_attn_floor_W"], np.float32),
        "w_routerT": np.ascontiguousarray(
            np.asarray(i["L0_ffn_w_router"]).T, np.float32),
        "w_upT": np.ascontiguousarray(
            np.asarray(i["L0_ffn_w_up"]).transpose(0, 2, 1), np.float32),
        "w_downT": np.ascontiguousarray(
            np.asarray(i["L0_ffn_w_down"]).transpose(0, 2, 1), np.float32),
        "conv_q_pt": np.ascontiguousarray(
            np.asarray(i["L0_attn_conv_q_w"]).reshape(8, 128, 4)
            .transpose(1, 0, 2).reshape(128, 32), np.float32),
        "conv_k_pt": np.ascontiguousarray(
            np.asarray(i["L0_attn_conv_k_w"]).reshape(8, 128, 4)
            .transpose(1, 0, 2).reshape(128, 32), np.float32),
        "conv_v_pt": np.ascontiguousarray(
            np.asarray(i["L0_attn_conv_v_w"]).reshape(8, 128, 4)
            .transpose(1, 0, 2).reshape(128, 32), np.float32),
        "fir_s_pt": np.ascontiguousarray(
            np.asarray(i["L0_attn_fir_short_w"]).transpose(1, 0, 2)
            .reshape(128, 24), np.float32),
        "fir_l_pt": np.ascontiguousarray(
            np.asarray(i["L0_attn_fir_long_w"]).transpose(1, 0, 2)
            .reshape(128, 248), np.float32),
        "fus_b1_pt": np.ascontiguousarray(
            np.asarray(i["L0_attn_fus_b1"]).reshape(16, 128).T, np.float32),
        "fus_b2_c": np.ascontiguousarray(
            np.asarray(i["L0_attn_fus_b2"]).reshape(32, 1), np.float32),
        "floor_b_c": np.ascontiguousarray(
            np.asarray(i["L0_attn_floor_b"]).reshape(NH, 1), np.float32),
        "norm_pf_r": np.ascontiguousarray(
            np.asarray(i["L0_attn_norm_pf_w"]).reshape(1, 128), np.float32),
        "norm_o_r": np.ascontiguousarray(
            np.asarray(i["L0_attn_norm_o_w"]).reshape(1, 128), np.float32),
        "gamma_r": np.ascontiguousarray(
            np.asarray(i["L0_ffn_norm_scale"]).reshape(1, H), np.float32),
    }
    hs = np.ascontiguousarray(np.asarray(i["L0_attn_hidden_states"]),
                              dtype=np.float32)
    in_maps = []
    for c in range(NCORES):
        m = dict(base)
        m["x"] = np.ascontiguousarray(
            hs[c * B_CORE:(c + 1) * B_CORE].reshape(TOK, H))
        in_maps.append(m)
    return in_maps


def kernel(**inputs):
    nc = build()
    in_maps = make_inputs(inputs)
    res = run_bass_kernel_spmd(nc, in_maps, core_ids=list(range(NCORES)))
    out = np.concatenate([r["out"].reshape(B_CORE, L, H)
                          for r in res.results], axis=0)
    return out
